# version 07
# baseline (speedup 1.0000x reference)
import numpy as np
import ml_dtypes
import concourse.bass as bass
import concourse.mybir as mybir
from concourse.bass_utils import run_bass_kernel_spmd

F32 = mybir.dt.float32
BF16 = mybir.dt.bfloat16
AF = mybir.ActivationFunctionType
ALU = mybir.AluOpType
AX = mybir.AxisListType
BFD = ml_dtypes.bfloat16

E = 2048
H = 16
KVH = 4
D = 128
HG = 12
SYM = 4
NF = 8192
NEC = 16
NEGC = 14.0
INV_SQRT_D = float(1.0 / np.sqrt(np.float32(128.0)))

ENGS = ("sync", "vector", "scalar", "tensor", "gpsimd")


class Sem:
    def __init__(self, name):
        self.name = name
        self.n = 0
        self.h = None


def build(S, n_dbg=0, STAG=2):
    NT = S // 128
    NSC = S // 512
    nc = bass.Bass("TRN2", target_bir_lowering=False, debug=False)

    d_x = nc.dram_tensor("x", [S, E], F32, kind="ExternalInput").ap()
    d_xT = nc.dram_tensor("xT", [E, S], F32, kind="ExternalInput").ap()
    d_wq = nc.dram_tensor("wq", [H, 128, NEC, 128], F32, kind="ExternalInput").ap()
    d_wk = nc.dram_tensor("wk", [KVH, 128, NEC, 128], F32, kind="ExternalInput").ap()
    d_wv = nc.dram_tensor("wv", [E, 512], F32, kind="ExternalInput").ap()
    d_wrq = nc.dram_tensor("wrq", [3, 128, NEC, 128], F32, kind="ExternalInput").ap()
    d_wrk = nc.dram_tensor("wrk", [3, 128, NEC, 128], F32, kind="ExternalInput").ap()
    d_wo = nc.dram_tensor("wo", [16, 128, NEC, 128], F32, kind="ExternalInput").ap()
    d_w0 = nc.dram_tensor("w0", [64, 128, NEC, 128], F32, kind="ExternalInput").ap()
    d_w1 = nc.dram_tensor("w1", [16, 128, 64, 128], F32, kind="ExternalInput").ap()
    d_qnw = nc.dram_tensor("qnw", [128, 2], F32, kind="ExternalInput").ap()
    d_knw = nc.dram_tensor("knw", [128, 2], F32, kind="ExternalInput").ap()
    d_cos = nc.dram_tensor("costab", [128, S], F32, kind="ExternalInput").ap()
    d_sin = nc.dram_tensor("sintab", [128, S], F32, kind="ExternalInput").ap()
    d_rotm = nc.dram_tensor("rotm", [128, 128], BF16, kind="ExternalInput").ap()
    d_ident = nc.dram_tensor("ident", [128, 128], BF16, kind="ExternalInput").ap()
    d_mask = nc.dram_tensor("maskbuf", [128, 896], BF16, kind="ExternalInput").ap()
    d_gs = nc.dram_tensor("gs2", [128, 3], F32, kind="ExternalInput").ap()
    d_b0 = nc.dram_tensor("b0r", [128, 64], F32, kind="ExternalInput").ap()
    d_b1 = nc.dram_tensor("b1r", [128, 16], F32, kind="ExternalInput").ap()
    d_cst = nc.dram_tensor("cst", [128, 8], F32, kind="ExternalInput").ap()
    d_onesc = nc.dram_tensor("onesc", [128, 1], BF16, kind="ExternalInput").ap()
    d_onesr = nc.dram_tensor("onesr", [1, 128], BF16, kind="ExternalInput").ap()
    d_onesm = nc.dram_tensor("onesm", [128, 128], BF16, kind="ExternalInput").ap()
    d_y1sp = nc.dram_tensor("y1sp", [E, S], BF16, kind="Internal").ap()
    d_wbf = nc.dram_tensor("wbf", [160, 128, NEC, 128], BF16, kind="Internal").ap()
    d_out = nc.dram_tensor("outT", [E, S], F32, kind="ExternalOutput").ap()

    C_NEGC, C_EPSLN, C_EPSRMS, C_INV_E, C_INV_D, C_HALF, C_ZERO, C_ONE = range(8)

    KT_O = 0
    V_O = KT_O + 4 * S
    RSQ_O = V_O + NT * 512
    RSK_O = RSQ_O + 3 * S
    QT_O = RSK_O + 3 * S
    AT_O = QT_O + 8 * 512
    PB_O = AT_O + 16 * 512
    AM_O = PB_O + 4 * 512
    TB_O = AM_O + 9 * 512
    HACT_O = 0
    W1S_O = 64 * 512
    Y1T_O = W1S_O + 4 * 2048
    ARENA = max(TB_O + 2 * 512, Y1T_O + 2 * 512)
    SQ_O = 0
    XN_O = 2048
    TK_O = QT_O

    AM_QSQ = AM_O
    AM_QRAW = AM_O + 512
    AM_T1 = AM_O + 2 * 512
    AM_RSBC = AM_O + 3 * 512
    AM_RQS = AM_O + 4 * 512
    AM_RKS = AM_O + 5 * 512
    AM_Y1 = AM_O + 6 * 512
    AM_RS2 = AM_O + 8 * 512

    prog = {e: [] for e in ENGS}
    sLD = Sem("sLD")
    sST = Sem("sST")
    sL2 = Sem("sL2")
    sPE = Sem("sPE")
    sAC = Sem("sAC")
    sDV = Sem("sDV")
    sGP = Sem("sGP")
    sems = [sLD, sST, sL2, sPE, sAC, sDV, sGP]

    def op(eng, fn, *waits):
        prog[eng].append((tuple(waits), fn))

    def dma(q, dst, src, *waits):
        sem = {"sync": sLD, "gpsimd": sST, "scalar": sL2}[q]
        def fn(e, dst=dst, src=src, sem=sem):
            e.dma_start(dst, src).then_inc(sem.h, 16)
        op(q, fn, (sem, sem.n), *waits)
        sem.n += 16
        return sem.n

    def pe(fn, *waits, inc=False):
        if inc:
            def fn2(e, fn=fn):
                fn(e).then_inc(sPE.h, 1)
            op("tensor", fn2, *waits)
            sPE.n += 1
            return sPE.n
        op("tensor", fn, *waits)
        return sPE.n

    def flush(eng, *waits):
        sem = {"vector": sDV, "scalar": sAC, "gpsimd": sGP}[eng]
        def fn(e, sem=sem):
            e.drain().then_inc(sem.h, 1)
        op(eng, fn, *waits)
        sem.n += 1
        return sem.n

    import contextlib
    cx = contextlib.ExitStack()
    EC = cx.enter_context
    xnT = EC(nc.sbuf_tensor("xnT", [128, NEC * S], BF16))
    ar = EC(nc.sbuf_tensor("ar", [128, ARENA], BF16))
    wcbf = EC(nc.sbuf_tensor("wcbf", [128, 2, 2048], F32))
    wcbb = EC(nc.sbuf_tensor("wcbb", [128, 2 * 2048], BF16))
    fst = EC(nc.sbuf_tensor("fst", [128, 3 * 512], F32))
    xtb = EC(nc.sbuf_tensor("xtb", [128, 2 * 512], F32))
    rowf = EC(nc.sbuf_tensor("rowf", [128, 1024], F32))
    rbf = EC(nc.sbuf_tensor("rbf", [1, 512], BF16))
    cstS = EC(nc.sbuf_tensor("cstS", [128, 8], F32))
    qnwS = EC(nc.sbuf_tensor("qnwS", [128, 2], F32))
    knwS = EC(nc.sbuf_tensor("knwS", [128, 2], F32))
    gsS = EC(nc.sbuf_tensor("gsS", [128, 4], F32))
    b0S = EC(nc.sbuf_tensor("b0S", [128, 64], F32))
    b1S = EC(nc.sbuf_tensor("b1S", [128, 16], F32))
    lnS = EC(nc.sbuf_tensor("lnS", [128, 8], F32))
    rotmS = EC(nc.sbuf_tensor("rotmS", [128, 128], BF16))
    identS = EC(nc.sbuf_tensor("identS", [128, 128], BF16))
    maskS = EC(nc.sbuf_tensor("maskS", [128, 896], BF16))
    onescS = EC(nc.sbuf_tensor("onescS", [128, 1], BF16))
    onesrS = EC(nc.sbuf_tensor("onesrS", [1, 128], BF16))
    onesmS = EC(nc.sbuf_tensor("onesmS", [128, 128], BF16))
    ps = [EC(nc.psum_tensor("ps%d" % i, [128, 512], F32)) for i in range(8)]

    wstate = {"wfi": 0}
    wslot_free = [(sDV, 0), (sDV, 0)]
    WSP = {}

    def feed32(src_ap, width=2048, q="sync", *extra):
        sl = wstate["wfi"] % 2
        wstate["wfi"] += 1
        v = dma(q, wcbf[:, sl, 0:width], src_ap, wslot_free[sl], *extra)
        return sl, v

    v0 = dma("sync", cstS[:, :], d_cst[:, :])
    dma("sync", qnwS[:, :], d_qnw[:, :])
    dma("sync", knwS[:, :], d_knw[:, :])
    dma("sync", gsS[:, 0:3], d_gs[:, :])
    dma("sync", b0S[:, :], d_b0[:, :])
    dma("sync", b1S[:, :], d_b1[:, :])
    dma("sync", rotmS[:, :], d_rotm[:, :])
    dma("sync", identS[:, :], d_ident[:, :])
    dma("sync", maskS[:, :], d_mask[:, :])
    dma("sync", onescS[:, :], d_onesc[:, :])
    dma("sync", onesrS[:, :], d_onesr[:, :])
    v_const = dma("sync", onesmS[:, :], d_onesm[:, :])
    def f_gshalf(e):
        e.tensor_scalar(gsS[:, 3:4], gsS[:, 0:1], 0.5, None, ALU.mult)
    op("vector", f_gshalf, (sLD, v_const))
    v_gs = flush("vector")

    xn_done_pe = 0
    last_norm_dv = 0
    for st in range(NT):
        xsl, v_x = feed32(d_x[st * 128:(st + 1) * 128, :])
        def f_red(e, xsl=xsl):
            e.reduce_sum(lnS[:, 0:1], wcbf[:, xsl, :], axis=AX.X)
        op("vector", f_red, (sLD, v_x))
        v_red = flush("vector")
        def f_sq(e, xsl=xsl):
            e.activation(ar[:, SQ_O:SQ_O + 2048], wcbf[:, xsl, :], AF.Square,
                         accum_out=lnS[:, 1:2])
        op("scalar", f_sq, (sLD, v_x))
        v_sq = flush("scalar")
        def f_ln1(e):
            e.tensor_scalar(lnS[:, 2:3], lnS[:, 0:1], 1.0 / 2048.0, None, ALU.mult)
            e.tensor_scalar(lnS[:, 3:4], lnS[:, 1:2], 1.0 / 2048.0,
                            cstS[:, C_EPSLN:C_EPSLN + 1], ALU.mult, ALU.add)
        op("vector", f_ln1, (sDV, v_red), (sAC, v_sq))
        flush("vector")
        def f_ln2(e):
            e.scalar_tensor_tensor(lnS[:, 4:5], lnS[:, 2:3], -1.0, lnS[:, 2:3],
                                   ALU.mult, ALU.mult)
        op("vector", f_ln2)
        flush("vector")
        def f_ln3(e):
            e.tensor_tensor(lnS[:, 5:6], lnS[:, 3:4], lnS[:, 4:5], ALU.add)
        op("vector", f_ln3)
        flush("vector")
        def f_ln4(e):
            e.reciprocal(lnS[:, 6:7], lnS[:, 5:6])
        op("vector", f_ln4)
        v_rec = flush("vector")
        def f_ln5(e):
            e.activation(lnS[:, 7:8], lnS[:, 6:7], AF.Sqrt)
        op("scalar", f_ln5, (sDV, v_rec))
        v_rs = flush("scalar")
        def f_ln6(e):
            e.scalar_tensor_tensor(lnS[:, 4:5], lnS[:, 2:3], -1.0, lnS[:, 7:8],
                                   ALU.mult, ALU.mult)
        op("vector", f_ln6, (sAC, v_rs))
        flush("vector")
        xslot = XN_O + (st % 2) * 2048
        def f_norm(e, xslot=xslot, xsl=xsl):
            e.tensor_scalar(ar[:, xslot:xslot + 2048], wcbf[:, xsl, :],
                            lnS[:, 7:8], lnS[:, 4:5], ALU.mult, ALU.add)
        op("vector", f_norm, (sPE, xn_done_pe), (sAC, v_sq))
        last_norm_dv = flush("vector")
        wslot_free[xsl] = (sDV, last_norm_dv)
        evac_dv = {}
        for g4 in range(4):
            bank = ps[6 + (g4 % 2)]
            w_ev = evac_dv.get(g4 - 2, 0)
            for k in range(4):
                ecx = g4 * 4 + k
                def f_tr(e, bank=bank, xslot=xslot, ecx=ecx, k=k):
                    return e.matmul(bank[:, k * 128:(k + 1) * 128],
                                    ar[:, xslot + ecx * 128:xslot + (ecx + 1) * 128],
                                    identS[:, :], start=True, stop=True)
                v_pe = pe(f_tr, (sDV, last_norm_dv), (sDV, w_ev),
                          inc=(k == 3))
            def f_ev(e, bank=bank, g4=g4, st=st):
                for k in range(4):
                    ecx = g4 * 4 + k
                    e.tensor_copy(xnT[:, ecx * S + st * 128: ecx * S + (st + 1) * 128],
                                  bank[:, k * 128:(k + 1) * 128])
            op("vector", f_ev, (sPE, v_pe))
            evac_dv[g4] = flush("vector")
        xn_done_pe = sPE.n

    v_xnT_dv = sDV.n

    csl, v_c = feed32(d_cos[:, :], S)
    def f_foldkc(e, csl=csl):
        e.tensor_scalar(ar[:, TK_O:TK_O + S], wcbf[:, csl, 0:S],
                        knwS[:, 0:1], None, ALU.mult)
    op("vector", f_foldkc, (sLD, v_c))
    v_fkc = flush("vector")
    wslot_free[csl] = (sDV, v_fkc)
    ssl, v_s = feed32(d_sin[:, :], S)
    def f_foldks(e, ssl=ssl):
        e.tensor_scalar(ar[:, TK_O + S:TK_O + 2 * S], wcbf[:, ssl, 0:S],
                        knwS[:, 1:2], None, ALU.mult)
    op("vector", f_foldks, (sLD, v_s))
    v_fks = flush("vector")
    wslot_free[ssl] = (sDV, v_fks)

    ps2_free = (sDV, 0)
    wcbb_free = [((sPE, 0),), ((sPE, 0),)]
    psum_free = [(sDV, 0), (sDV, 0)]
    cb_list = [("k", g) for g in range(KVH)] + \
              [("rq", rc) for rc in range(3)] + [("rk", rc) for rc in range(3)]
    it = 0
    for kind, idx in cb_list:
        src = {"k": d_wk, "rq": d_wrq, "rk": d_wrk}[kind]
        fsl, v_w = feed32(src[idx, :, :, :])
        slot = it % 2
        def f_cast(e, slot=slot, fsl=fsl):
            e.tensor_copy(wcbb[:, slot * 2048:(slot + 1) * 2048], wcbf[:, fsl, :])
        op("vector", f_cast, (sLD, v_w), *wcbb_free[slot])
        v_cast = flush("vector")
        wslot_free[fsl] = (sDV, v_cast)
        for tc in range(NSC):
            par = (it * NSC + tc) % 2
            bank = ps[par]
            for ec in range(NEC):
                def f_mm(e, bank=bank, slot=slot, ec=ec, tc=tc):
                    return e.matmul(bank[:, :],
                                    wcbb[:, slot * 2048 + ec * 128:
                                         slot * 2048 + (ec + 1) * 128],
                                    xnT[:, ec * S + tc * 512: ec * S + tc * 512 + 512],
                                    start=(ec == 0), stop=(ec == NEC - 1))
                v_mm = pe(f_mm, (sDV, v_cast), psum_free[par],
                          (sDV, v_xnT_dv) if ec == 0 and it == 0 else (sDV, 0),
                          inc=(ec == NEC - 1))
            if kind == "k":
                g, tcp = idx, tc
                def f_ksq(e, bank=bank):
                    e.activation(ar[:, AM_QSQ:AM_QSQ + 512], bank[:, :], AF.Square)
                op("scalar", f_ksq, (sPE, v_mm))
                v_ksq = flush("scalar")
                def f_dmm(e):
                    return e.matmul(ps[2][:, :], onesmS[:, :],
                                    ar[:, AM_QSQ:AM_QSQ + 512], start=True, stop=True)
                v_dmm = pe(f_dmm, (sAC, v_ksq), ps2_free, inc=True)
                def f_r1(e):
                    e.activation(rowf[:, 0:512], ps[2][:, :], AF.Sqrt,
                                 bias=cstS[:, C_EPSRMS:C_EPSRMS + 1],
                                 scale=cstS[:, C_INV_D:C_INV_D + 1])
                op("scalar", f_r1, (sPE, v_dmm))
                ps2_free = (sAC, flush("scalar"))
                def f_r3(e):
                    e.reciprocal(rowf[:, 512:1024], rowf[:, 0:512])
                op("vector", f_r3, ps2_free)
                v_r3 = flush("vector")
                def f_kraw(e, bank=bank):
                    e.tensor_tensor(ar[:, AM_QRAW:AM_QRAW + 512], bank[:, :],
                                    rowf[:, 512:1024], ALU.mult)
                op("vector", f_kraw, (sDV, v_r3))
                v_kraw = flush("vector")
                psum_free[par] = (sDV, v_kraw)
                def f_rot(e):
                    return e.matmul(ps[3][:, :], rotmS[:, :],
                                    ar[:, AM_QRAW:AM_QRAW + 512], start=True, stop=True)
                v_rot = pe(f_rot, (sDV, v_kraw), inc=True)
                def f_t1(e, tcp=tcp):
                    e.tensor_tensor(ar[:, AM_T1:AM_T1 + 512],
                                    ar[:, AM_QRAW:AM_QRAW + 512],
                                    ar[:, TK_O + tcp * 512:TK_O + tcp * 512 + 512],
                                    ALU.mult)
                op("vector", f_t1)
                flush("vector")
                def f_t2(e, tcp=tcp):
                    e.tensor_tensor(ar[:, AM_QRAW:AM_QRAW + 512], ps[3][:, :],
                                    ar[:, TK_O + S + tcp * 512:TK_O + S + tcp * 512 + 512],
                                    ALU.mult)
                op("vector", f_t2, (sPE, v_rot))
                flush("vector")
                def f_kT(e, g=g, tcp=tcp):
                    e.tensor_tensor(ar[:, KT_O + g * S + tcp * 512:
                                       KT_O + g * S + tcp * 512 + 512],
                                    ar[:, AM_T1:AM_T1 + 512],
                                    ar[:, AM_QRAW:AM_QRAW + 512], ALU.add)
                op("vector", f_kT)
                flush("vector")
            else:
                dst_o = (RSQ_O if kind == "rq" else RSK_O) + idx * S + tc * 512
                def f_ev(e, bank=bank, dst_o=dst_o):
                    e.tensor_copy(ar[:, dst_o:dst_o + 512], bank[:, :])
                op("vector", f_ev, (sPE, v_mm))
                psum_free[par] = (sDV, flush("vector"))
        wcbb_free[slot] = ((sPE, sPE.n),)
        it += 1

    vps_free = [(sAC, 0), (sAC, 0), (sDV, v_xnT_dv), (sDV, v_xnT_dv)]
    for ttg in range(NT // 4):
        for ec in range(NEC):
            fsl, v_w = feed32(d_wv[ec * 128:(ec + 1) * 128, :], 512)
            slot = ec % 2
            def f_cast(e, slot=slot, fsl=fsl):
                e.tensor_copy(wcbb[:, slot * 2048:slot * 2048 + 512],
                              wcbf[:, fsl, 0:512])
            op("vector", f_cast, (sLD, v_w), *wcbb_free[slot])
            v_cast = flush("vector")
            wslot_free[fsl] = (sDV, v_cast)
            for j in range(4):
                tt = ttg * 4 + j
                def f_mm(e, j=j, slot=slot, ec=ec, tt=tt):
                    return e.matmul(ps[4 + j][:, :],
                                    xnT[:, ec * S + tt * 128: ec * S + (tt + 1) * 128],
                                    wcbb[:, slot * 2048:slot * 2048 + 512],
                                    start=(ec == 0), stop=(ec == NEC - 1))
                v_mm = pe(f_mm, (sDV, v_cast), vps_free[j],
                          inc=(j == 3 or ec == NEC - 1))
            wcbb_free[slot] = ((sPE, sPE.n),)
        for j in range(4):
            tt = ttg * 4 + j
            def f_ev(e, j=j, tt=tt):
                e.activation(ar[:, V_O + tt * 512:V_O + (tt + 1) * 512],
                             ps[4 + j][:, :], AF.Copy)
            op("scalar", f_ev, (sPE, v_mm))
            vps_free[j] = (sAC, flush("scalar"))

    a0_end_pe = sPE.n
    a0_end_dv = sDV.n
    a0_end_ac = sAC.n

    y1_spills = []
    wcbb_free_m = wcbb_free
    tab_dv = 0
    pslot_free = [(sPE, 0)] * 4
    qps_free = [(sAC, a0_end_ac), (sAC, a0_end_ac)]
    sc_ps_free = [(sDV, a0_end_dv), (sDV, a0_end_dv)]
    av_free = [(sAC, a0_end_ac), (sAC, a0_end_ac)]
    qT_free = [(sPE, 0)] * 8
    y1st_free = [(sST, 0), (sST, 0)]
    rqs_free = (sPE, 0)
    ps3_free = (sDV, 0)
    pend_end = None
    rope_q = []
    rdn_free = [(sDV, 0)]
    rs_free = [(sDV, 0), (sDV, 0), (sDV, 0)]
    rowf0_free = (sDV, 0)
    spill_q = []
    for j in range(64):
        spill_q.append((j, lambda j=j: d_w0[j, :, :, :]))
    for eoc in range(16):
        for qq in range(4):
            spill_q.append((64 + eoc * 4 + qq,
                            lambda eoc=eoc, qq=qq: d_w1[eoc, :, qq * 16:(qq + 1) * 16, :]))
    spill_i = [0]
    sb_slot = [0]

    def emit_spills(k):
        for _ in range(k):
            if spill_i[0] >= len(spill_q):
                return
            blk, srcfn = spill_q[spill_i[0]]
            spill_i[0] += 1
            fsl, v_w = feed32(srcfn(), 2048, "scalar")
            bsl = sb_slot[0] % 2
            sb_slot[0] += 1
            def f_cast(e, bsl=bsl, fsl=fsl):
                e.tensor_copy(wcbb[:, bsl * 2048:(bsl + 1) * 2048], wcbf[:, fsl, :])
            op("vector", f_cast, (sL2, v_w), *wcbb_free_m[bsl])
            v_cast = flush("vector")
            wslot_free[fsl] = (sDV, v_cast)
            WSP[blk] = dma("gpsimd", d_wbf[blk, :, :, :],
                           wcbb[:, bsl * 2048:(bsl + 1) * 2048], (sDV, v_cast))
            wcbb_free_m[bsl] = ((sST, WSP[blk]),)
    aT_read_pe = 0
    for sc in range(NSC):
        nt = 4 * (sc + 1)
        v_c = dma("sync", fst[:, 0:512], d_cos[:, sc * 512:(sc + 1) * 512],
                  (sDV, sDV.n))
        def f_fqc(e):
            e.tensor_scalar(ar[:, TB_O:TB_O + 512], fst[:, 0:512],
                            qnwS[:, 0:1], INV_SQRT_D, ALU.mult, ALU.mult)
        op("vector", f_fqc, (sLD, v_c))
        v_fq = flush("vector")
        v_s2 = dma("sync", fst[:, 0:512], d_sin[:, sc * 512:(sc + 1) * 512],
                   (sDV, v_fq))
        def f_fqs(e):
            e.tensor_scalar(ar[:, TB_O + 512:TB_O + 1024], fst[:, 0:512],
                            qnwS[:, 1:2], INV_SQRT_D, ALU.mult, ALU.mult)
        op("vector", f_fqs, (sLD, v_s2))
        tab_dv = flush("vector")

        for half in range(2):
            for hh in range(8):
                h = half * 8 + hh
                slot = h % 2
                if sc == 0:
                    fsl, v_w = feed32(d_wq[h, :, :, :])
                    def f_cast(e, slot=slot, fsl=fsl):
                        e.tensor_copy(wcbb[:, slot * 2048:(slot + 1) * 2048],
                                      wcbf[:, fsl, :])
                    op("vector", f_cast, (sLD, v_w), *wcbb_free_m[slot])
                    v_cast = flush("vector")
                    wslot_free[fsl] = (sDV, v_cast)
                    WSP[128 + h] = dma("gpsimd", d_wbf[128 + h, :, :, :],
                                       wcbb[:, slot * 2048:(slot + 1) * 2048],
                                       (sDV, v_cast))
                    mm_w = (sDV, v_cast)
                else:
                    v_w = dma("sync", wcbb[:, slot * 2048:(slot + 1) * 2048],
                              d_wbf[128 + h, :, :, :], *wcbb_free_m[slot],
                              (sST, WSP[128 + h]))
                    mm_w = (sLD, v_w)
                par = h % 2
                bank = ps[6 + par]
                for ec in range(NEC):
                    def f_mm(e, bank=bank, slot=slot, ec=ec, sc=sc):
                        return e.matmul(bank[:, :],
                                        wcbb[:, slot * 2048 + ec * 128:
                                             slot * 2048 + (ec + 1) * 128],
                                        xnT[:, ec * S + sc * 512: ec * S + sc * 512 + 512],
                                        start=(ec == 0), stop=(ec == NEC - 1))
                    v_mm = pe(f_mm, mm_w, qps_free[par],
                              inc=(ec == NEC - 1))
                wcbb_free_m[slot] = (((sPE, sPE.n),) if sc > 0 else
                                     ((sPE, sPE.n), (sST, WSP[128 + h])))
                def f_qsq(e, bank=bank):
                    e.activation(ar[:, AM_QSQ:AM_QSQ + 512], bank[:, :], AF.Square)
                op("scalar", f_qsq, (sPE, v_mm), (sDV, tab_dv))
                v_qsq = flush("scalar")
                def f_dmm(e):
                    return e.matmul(ps[2][:, :], onesmS[:, :],
                                    ar[:, AM_QSQ:AM_QSQ + 512], start=True, stop=True)
                v_dmm = pe(f_dmm, (sAC, v_qsq), ps2_free, inc=True)
                def f_r1(e):
                    e.activation(rowf[:, 0:512], ps[2][:, :], AF.Sqrt,
                                 bias=cstS[:, C_EPSRMS:C_EPSRMS + 1],
                                 scale=cstS[:, C_INV_D:C_INV_D + 1])
                op("scalar", f_r1, (sPE, v_dmm), rowf0_free)
                ps2_free = (sAC, flush("scalar"))
                rs_o = (hh % 3) * 512
                def f_r3(e, rs_o=rs_o):
                    e.reciprocal(fst[:, rs_o:rs_o + 512], rowf[:, 0:512])
                op("vector", f_r3, ps2_free, rs_free[hh % 3])
                v_r3 = flush("vector")
                rowf0_free = (sDV, v_r3)

                def mk_rope(hh=hh, h=h, bank=bank, par=par, rs_o=rs_o, v_r3=v_r3):
                    def doit():
                        nonlocal qps_free
                        def f_qraw(e, bank=bank, rs_o=rs_o):
                            e.tensor_tensor(ar[:, AM_QRAW:AM_QRAW + 512], bank[:, :],
                                            fst[:, rs_o:rs_o + 512], ALU.mult)
                        op("vector", f_qraw, (sDV, v_r3))
                        rs_free[rs_o // 512] = (sDV, sDV.n + 1)
                        v_qraw = flush("vector")
                        qps_free[par] = (sDV, v_qraw)
                        def f_rot(e):
                            return e.matmul(ps[3][:, :], rotmS[:, :],
                                            ar[:, AM_QRAW:AM_QRAW + 512],
                                            start=True, stop=True)
                        v_rot = pe(f_rot, (sDV, v_qraw), inc=True)
                        def f_t1(e):
                            e.tensor_tensor(ar[:, AM_T1:AM_T1 + 512],
                                            ar[:, AM_QRAW:AM_QRAW + 512],
                                            ar[:, TB_O:TB_O + 512], ALU.mult)
                        op("vector", f_t1)
                        flush("vector")
                        def f_t2(e):
                            e.tensor_tensor(ar[:, AM_QRAW:AM_QRAW + 512], ps[3][:, :],
                                            ar[:, TB_O + 512:TB_O + 1024], ALU.mult)
                        op("vector", f_t2, (sPE, v_rot))
                        flush("vector")
                        def f_qT(e, hh=hh):
                            e.tensor_tensor(ar[:, QT_O + hh * 512:QT_O + hh * 512 + 512],
                                            ar[:, AM_T1:AM_T1 + 512],
                                            ar[:, AM_QRAW:AM_QRAW + 512], ALU.add)
                        op("vector", f_qT, qT_free[hh])
                        qT_free[hh] = (sDV, flush("vector"))
                    return doit
                rope_q.append(mk_rope())
                if len(rope_q) > 1:
                    rope_q.pop(0)()
            while rope_q:
                rope_q.pop(0)()

            for hh in range(8):
                h = half * 8 + hh
                g = h // 4
                jo = 32 * (h % 4)
                v_rqs = 0
                if h < HG:
                    gcol = 3 if h < SYM else h // 4
                    def f_rqs(e, h=h, jo=jo, gcol=gcol, sc=sc):
                        e.tensor_scalar(ar[jo:jo + 32, AM_RQS:AM_RQS + 512],
                                        ar[jo:jo + 32, RSQ_O + (h // 4) * S + sc * 512:
                                           RSQ_O + (h // 4) * S + sc * 512 + 512],
                                        gsS[jo:jo + 32, gcol:gcol + 1], None, ALU.mult)
                        if h < SYM:
                            e.tensor_scalar(ar[jo:jo + 32, AM_RKS:AM_RKS + 512],
                                            ar[jo:jo + 32, RSK_O + (h // 4) * S + sc * 512:
                                               RSK_O + (h // 4) * S + sc * 512 + 512],
                                            gsS[jo:jo + 32, gcol:gcol + 1], None, ALU.mult)
                    op("vector", f_rqs, rqs_free, (sDV, v_gs), qT_free[hh])
                    v_rqs = flush("vector")
                if pend_end is not None:
                    pend_end()
                    pend_end = None
                av_q = []
                dn = {}
                for tt in range(nt):
                    par = tt % 2
                    bank = ps[par]
                    def f_qk(e, bank=bank, g=g, tt=tt, hh=hh, h=h):
                        return e.matmul(bank[:, :],
                                        ar[:, KT_O + g * S + tt * 128:
                                           KT_O + g * S + (tt + 1) * 128],
                                        ar[:, QT_O + hh * 512:QT_O + hh * 512 + 512],
                                        start=True, stop=(h >= HG))
                    v_sc = pe(f_qk, sc_ps_free[par], qT_free[hh],
                              (sDV, v_rqs) if h < HG else (sDV, 0),
                              inc=(h >= HG))
                    if h < HG:
                        issym = h < SYM
                        def f_b1m(e, bank=bank, h=h, jo=jo, tt=tt, issym=issym):
                            return e.matmul(bank[:, :],
                                            ar[jo:jo + 32, RSK_O + (h // 4) * S + tt * 128:
                                               RSK_O + (h // 4) * S + (tt + 1) * 128],
                                            ar[jo:jo + 32, AM_RQS:AM_RQS + 512],
                                            start=False, stop=(not issym),
                                            tile_position=(jo, 0))
                        v_sc = pe(f_b1m, inc=(not issym))
                        if issym:
                            def f_b2m(e, bank=bank, h=h, jo=jo, tt=tt):
                                return e.matmul(bank[:, :],
                                                ar[jo:jo + 32, RSQ_O + (h // 4) * S + tt * 128:
                                                   RSQ_O + (h // 4) * S + (tt + 1) * 128],
                                                ar[jo:jo + 32, AM_RKS:AM_RKS + 512],
                                                start=False, stop=True,
                                                tile_position=(jo, 0))
                            v_sc = pe(f_b2m, inc=True)
                    slot = tt % 4
                    def f_exp(e, bank=bank, slot=slot):
                        e.activation(ar[:, PB_O + slot * 512:PB_O + (slot + 1) * 512],
                                     bank[:, :], AF.Exp, bias=cstS[:, C_NEGC:C_NEGC + 1])
                    op("scalar", f_exp, (sPE, v_sc), pslot_free[slot])
                    v_exp = flush("scalar")
                    sc_ps_free[par] = (sAC, v_exp)
                    p_ready = (sAC, v_exp)
                    if tt >= 4 * sc:
                        off = 128 * tt - 512 * sc
                        def f_msk(e, slot=slot, off=off):
                            e.tensor_tensor(ar[:, PB_O + slot * 512:PB_O + (slot + 1) * 512],
                                            ar[:, PB_O + slot * 512:PB_O + (slot + 1) * 512],
                                            maskS[:, 384 - off:384 - off + 512], ALU.mult)
                        op("gpsimd", f_msk, (sAC, v_exp))
                        p_ready = (sGP, flush("gpsimd"))
                    avb = ps[4 + (h % 2)]
                    def mk_avden(tt=tt, slot=slot, p_ready=p_ready, nt=nt,
                                 avb=avb, g=g, h=h):
                        def doit():
                            def f_av(e, avb=avb, tt=tt, g=g, slot=slot, nt=nt):
                                return e.matmul(avb[:, :],
                                                ar[:, V_O + tt * 512 + g * 128:
                                                   V_O + tt * 512 + (g + 1) * 128],
                                                ar[:, PB_O + slot * 512:PB_O + (slot + 1) * 512],
                                                start=(tt == 0), stop=(tt == nt - 1))
                            pe(f_av, p_ready, av_free[h % 2] if tt == 0 else (sPE, 0))
                            def f_den(e, tt=tt, slot=slot, nt=nt):
                                return e.matmul(ps[2][:, :], onesmS[:, :],
                                                ar[:, PB_O + slot * 512:PB_O + (slot + 1) * 512],
                                                start=(tt == 0), stop=(tt == nt - 1))
                            v = pe(f_den, ps2_free if tt == 0 else (sPE, 0), inc=True)
                            pslot_free[slot] = (sPE, v)
                            dn["v"] = v
                        return doit
                    av_q.append(mk_avden())
                    if tt >= 2:
                        av_q[tt - 2]()
                if nt >= 2:
                    av_q[nt - 2]()
                av_q[nt - 1]()
                v_avden = dn["v"]
                def mk_end(h=h, avb=avb, v_avden=v_avden):
                    def doit():
                        nonlocal ps2_free
                        def f_rd1(e):
                            e.reciprocal(rowf[:, 512:1024], ps[2][:, :])
                        op("vector", f_rd1, (sPE, v_avden), rdn_free[0])
                        ps2_free = (sDV, flush("vector"))
                        def f_aT(e, h=h, avb=avb):
                            e.tensor_tensor(ar[:, AT_O + h * 512:AT_O + h * 512 + 512],
                                            avb[:, :], rowf[:, 512:1024], ALU.mult)
                        op("vector", f_aT, ps2_free)
                        v_aTf = flush("vector")
                        av_free[h % 2] = (sDV, v_aTf)
                        rdn_free[0] = (sDV, v_aTf)
                    return doit
                pend_end = mk_end()
                rqs_free = (sPE, sPE.n)
                emit_spills([1, 2, 3, 3][sc] if S == 2048 else 8)
            if pend_end is not None:
                pend_end()
                pend_end = None
        v_aT_all = sDV.n

        for eoc in range(16):
            slot = eoc % 2
            if sc == 0:
                fsl, v_w = feed32(d_wo[eoc, :, :, :])
                def f_cast(e, slot=slot, fsl=fsl):
                    e.tensor_copy(wcbb[:, slot * 2048:(slot + 1) * 2048],
                                  wcbf[:, fsl, :])
                op("vector", f_cast, (sLD, v_w), *wcbb_free_m[slot])
                v_cast = flush("vector")
                wslot_free[fsl] = (sDV, v_cast)
                WSP[144 + eoc] = dma("gpsimd", d_wbf[144 + eoc, :, :, :],
                                     wcbb[:, slot * 2048:(slot + 1) * 2048],
                                     (sDV, v_cast))
                mm_w = (sDV, v_cast)
            else:
                v_w = dma("sync", wcbb[:, slot * 2048:(slot + 1) * 2048],
                          d_wbf[144 + eoc, :, :, :], *wcbb_free_m[slot],
                          (sST, WSP[144 + eoc]))
                mm_w = (sLD, v_w)
            par = eoc % 2
            bank = ps[6 + par]
            for ec in range(NEC):
                def f_mm(e, bank=bank, slot=slot, ec=ec):
                    return e.matmul(bank[:, :],
                                    wcbb[:, slot * 2048 + ec * 128:
                                         slot * 2048 + (ec + 1) * 128],
                                    ar[:, AT_O + ec * 512:AT_O + (ec + 1) * 512],
                                    start=(ec == 0), stop=(ec == NEC - 1))
                v_mm = pe(f_mm, mm_w, qps_free[par], (sDV, v_aT_all),
                          inc=(ec == NEC - 1))
            wcbb_free_m[slot] = (((sPE, sPE.n),) if sc > 0 else
                                 ((sPE, sPE.n), (sST, WSP[144 + eoc])))
            y1slot = AM_Y1 + (eoc % 2) * 512
            def f_ev(e, bank=bank, y1slot=y1slot):
                e.activation(ar[:, y1slot:y1slot + 512], bank[:, :], AF.Copy)
            op("scalar", f_ev, (sPE, v_mm), y1st_free[eoc % 2])
            v_ev = flush("scalar")
            qps_free[par] = (sAC, v_ev)
            v_sp = dma("gpsimd", d_y1sp[eoc * 128:(eoc + 1) * 128, sc * 512:sc * 512 + 512],
                       ar[:, y1slot:y1slot + 512], (sAC, v_ev))
            y1st_free[eoc % 2] = (sST, v_sp)
        aT_read_pe = sPE.n
    a_end_pe = sPE.n
    a_end_dv = sDV.n
    a_end_ac = sAC.n

    wcbb_free_b = wcbb_free_m
    y1sp_done = sST.n
    hact_free_pe = a_end_pe
    w0ps_free = [(sAC, a_end_ac), (sAC, a_end_ac)]
    w1ps_free = [(sDV, a_end_dv), (sDV, a_end_dv)]
    xtb_free = [(sDV, 0), (sDV, 0)]
    y1t_free = [(sDV, 0), (sDV, 0)]
    outst_free = [(sST, 0), (sST, 0)]
    for sc in range(NSC):
        for j in range(64):
            slot = j % 2
            v_w = dma("sync", wcbb[:, slot * 2048:(slot + 1) * 2048],
                      d_wbf[j, :, :, :], *wcbb_free_b[slot], (sST, WSP[j]),
                      (sPE, a_end_pe))
            par = j % 2
            bank = ps[par]
            for ec in range(NEC):
                def f_mm(e, bank=bank, slot=slot, ec=ec, sc=sc):
                    return e.matmul(bank[:, :],
                                    wcbb[:, slot * 2048 + ec * 128:
                                         slot * 2048 + (ec + 1) * 128],
                                    xnT[:, ec * S + sc * 512: ec * S + sc * 512 + 512],
                                    start=(ec == 0), stop=(ec == NEC - 1))
                v_mm = pe(f_mm, (sLD, v_w), w0ps_free[par], inc=(ec == NEC - 1))
            wcbb_free_b[slot] = ((sPE, sPE.n),)
            def f_gelu(e, bank=bank, j=j):
                e.activation(ar[:, HACT_O + j * 512:HACT_O + (j + 1) * 512],
                             bank[:, :], AF.Gelu, bias=b0S[:, j:j + 1])
            op("scalar", f_gelu, (sPE, v_mm), (sPE, hact_free_pe),
               (sST, y1sp_done) if sc == 0 and j == 0 else (sST, 0))
            w0ps_free[par] = (sAC, flush("scalar"))
        v_hact_ac = sAC.n

        for eoc in range(16):
            par = eoc % 2
            bank = ps[2 + par]
            for q in range(4):
                blk = 64 + eoc * 4 + q
                slot = (eoc * 4 + q) % 2
                v_w = dma("sync", wcbb[:, slot * 2048:(slot + 1) * 2048],
                          d_wbf[blk, :, :, :], *wcbb_free_b[slot],
                          (sST, WSP[blk]))
                for i in range(16):
                    ncb = q * 16 + i
                    def f_mm(e, bank=bank, slot=slot, i=i, ncb=ncb):
                        return e.matmul(bank[:, :],
                                        wcbb[:, slot * 2048 + i * 128:
                                             slot * 2048 + (i + 1) * 128],
                                        ar[:, HACT_O + ncb * 512:HACT_O + (ncb + 1) * 512],
                                        start=(ncb == 0), stop=(ncb == 63))
                    v_mm = pe(f_mm, (sLD, v_w), (sAC, v_hact_ac),
                              w1ps_free[par] if ncb == 0 else (sPE, 0),
                              inc=(i == 15 or ncb == 63))
                wcbb_free_b[slot] = ((sPE, sPE.n),)
            v_xt = dma("scalar", xtb[:, (eoc % 2) * 512:(eoc % 2) * 512 + 512],
                       d_xT[eoc * 128:(eoc + 1) * 128, sc * 512:sc * 512 + 512],
                       xtb_free[eoc % 2])
            y1o = Y1T_O + (eoc % 2) * 512
            v_y1 = dma("scalar", ar[:, y1o:y1o + 512],
                       d_y1sp[eoc * 128:(eoc + 1) * 128, sc * 512:sc * 512 + 512],
                       y1t_free[eoc % 2], (sPE, a_end_pe), (sST, y1sp_done))
            def f_f1(e, bank=bank, eoc=eoc, y1o=y1o):
                e.scalar_tensor_tensor(fst[:, 0:512], bank[:, :],
                                       b1S[:, eoc:eoc + 1], ar[:, y1o:y1o + 512],
                                       ALU.add, ALU.add)
            op("vector", f_f1, (sPE, v_mm), (sL2, v_y1))
            v_f1 = flush("vector")
            w1ps_free[par] = (sDV, v_f1)
            y1t_free[eoc % 2] = (sDV, v_f1)
            oslot = 512 + (eoc % 2) * 512
            def f_f2(e, eoc=eoc, oslot=oslot):
                e.tensor_tensor(fst[:, oslot:oslot + 512], fst[:, 0:512],
                                xtb[:, (eoc % 2) * 512:(eoc % 2) * 512 + 512], ALU.add)
            op("vector", f_f2, (sL2, v_xt), outst_free[eoc % 2])
            v_f2 = flush("vector")
            xtb_free[eoc % 2] = (sDV, v_f2)
            v_out = dma("gpsimd", d_out[eoc * 128:(eoc + 1) * 128, sc * 512:sc * 512 + 512],
                        fst[:, oslot:oslot + 512], (sDV, v_f2))
            outst_free[eoc % 2] = (sST, v_out)
        hact_free_pe = sPE.n

    def f_fin(e, v=sST.n):
        e.wait_ge(sST.h, v)
    prog["gpsimd"].append(((), f_fin))

    with cx:
        for s_ in sems:
            s_.h = cx.enter_context(nc.semaphore(s_.name))
        block = cx.enter_context(nc.Block())

        def run_stream(eng_name):
            def runner(e):
                hw = {}
                for waits, fn in prog[eng_name]:
                    for sem, val in waits:
                        if val > 0 and hw.get(sem.name, 0) < val:
                            e.wait_ge(sem.h, val)
                            hw[sem.name] = val
                    fn(e)
            return runner
        block.sync(run_stream("sync"))
        block.vector(run_stream("vector"))
        block.scalar(run_stream("scalar"))
        block.tensor(run_stream("tensor"))
        block.gpsimd(run_stream("gpsimd"))
    return nc


def _colblocks(WT, nb):
    K, N = WT.shape
    assert N == nb * 128
    return np.ascontiguousarray(WT.reshape(K // 128, 128, nb, 128).transpose(2, 1, 0, 3))


def _host_inputs(S, L0_attn_Wq, L0_attn_Wk, L0_attn_Wv, L0_attn_Wo,
                 L0_attn_q_norm_w, L0_attn_k_norm_w, L0_attn_Wrq, L0_attn_Wrk,
                 L0_attn_graph_scale, L0_ffn_W0, L0_ffn_bias0, L0_ffn_W1,
                 L0_ffn_bias1):
    f32 = np.float32
    base = {}
    base["wq"] = _colblocks(np.ascontiguousarray(L0_attn_Wq.T), 16)
    base["wk"] = _colblocks(np.ascontiguousarray(L0_attn_Wk.T), 4)
    base["wv"] = np.ascontiguousarray(L0_attn_Wv.T)
    base["wrq"] = _colblocks(np.ascontiguousarray(L0_attn_Wrq.T), 3)
    base["wrk"] = _colblocks(np.ascontiguousarray(L0_attn_Wrk.T), 3)
    base["wo"] = _colblocks(np.ascontiguousarray(L0_attn_Wo.T), 16)
    base["w0"] = _colblocks(np.ascontiguousarray(L0_ffn_W0.T), 64)
    base["w1"] = _colblocks(np.ascontiguousarray(L0_ffn_W1.T), 16)
    qw = np.asarray(L0_attn_q_norm_w, f32)
    kw = np.asarray(L0_attn_k_norm_w, f32)
    base["qnw"] = np.stack([qw, np.roll(qw, -64)], axis=1).astype(f32)
    base["knw"] = np.stack([kw, np.roll(kw, -64)], axis=1).astype(f32)
    expnt = np.arange(0, 128, 2, dtype=f32) / f32(128.0)
    invf = (f32(1.0) / np.power(f32(10000.0), expnt)).astype(f32)
    t = np.arange(S, dtype=f32)
    freqs = np.outer(t, invf).astype(f32)
    emb = np.concatenate([freqs, freqs], axis=1)
    base["costab"] = np.ascontiguousarray(np.cos(emb).T.astype(f32))
    base["sintab"] = np.ascontiguousarray(np.sin(emb).T.astype(f32))
    rotm = np.zeros((128, 128), np.float32)
    for d in range(64):
        rotm[d + 64, d] = -1.0
        rotm[d, d + 64] = 1.0
    base["rotm"] = rotm.astype(BFD)
    base["ident"] = np.eye(128, dtype=BFD)
    mp = np.arange(128)[:, None]
    mm_ = np.arange(896)[None, :]
    base["maskbuf"] = (mm_ >= mp + 384).astype(BFD)
    gs = np.asarray(L0_attn_graph_scale, f32)
    gs2 = np.zeros((128, 3), f32)
    for c in range(3):
        gs2[:, c] = gs[4 * c + np.arange(128) // 32]
    base["gs2"] = gs2
    base["b0r"] = np.ascontiguousarray(np.asarray(L0_ffn_bias0, f32).reshape(64, 128).T)
    base["b1r"] = np.ascontiguousarray(np.asarray(L0_ffn_bias1, f32).reshape(16, 128).T)
    cst = np.zeros((128, 8), f32)
    cst[:, 0] = -NEGC
    cst[:, 1] = 1e-5
    cst[:, 2] = 1e-6
    cst[:, 3] = 1.0 / 2048.0
    cst[:, 4] = 1.0 / 128.0
    cst[:, 5] = 0.5
    cst[:, 6] = 0.0
    cst[:, 7] = 1.0
    base["cst"] = cst
    base["onesc"] = np.ones((128, 1), BFD)
    base["onesr"] = np.ones((1, 128), BFD)
    base["onesm"] = np.ones((128, 128), BFD)
    return base


def kernel(L0_attn_x, L0_attn_Wq, L0_attn_Wk, L0_attn_Wv, L0_attn_Wo,
           L0_attn_q_norm_w, L0_attn_k_norm_w, L0_attn_Wrq, L0_attn_Wrk,
           L0_attn_graph_scale, L0_ffn_W0, L0_ffn_bias0, L0_ffn_W1, L0_ffn_bias1):
    S = L0_attn_x.shape[1]
    base = _host_inputs(S, L0_attn_Wq, L0_attn_Wk, L0_attn_Wv, L0_attn_Wo,
                        L0_attn_q_norm_w, L0_attn_k_norm_w, L0_attn_Wrq,
                        L0_attn_Wrk, L0_attn_graph_scale, L0_ffn_W0,
                        L0_ffn_bias0, L0_ffn_W1, L0_ffn_bias1)
    nc = build(S)
    B = L0_attn_x.shape[0]
    in_maps = []
    for c in range(B):
        m = dict(base)
        xc = np.ascontiguousarray(np.asarray(L0_attn_x[c], np.float32))
        m["x"] = xc
        m["xT"] = np.ascontiguousarray(xc.T)
        in_maps.append(m)
    res = run_bass_kernel_spmd(nc, in_maps, core_ids=list(range(B)))
    out = np.stack([np.ascontiguousarray(r["outT"].T) for r in res.results], axis=0)
    return out


# version 08
# speedup vs baseline: 1.0092x; 1.0092x over previous
import numpy as np
import ml_dtypes
import concourse.bass as bass
import concourse.mybir as mybir
from concourse.bass_utils import run_bass_kernel_spmd

F32 = mybir.dt.float32
BF16 = mybir.dt.bfloat16
AF = mybir.ActivationFunctionType
ALU = mybir.AluOpType
AX = mybir.AxisListType
BFD = ml_dtypes.bfloat16

E = 2048
H = 16
KVH = 4
D = 128
HG = 12
SYM = 4
NF = 8192
NEC = 16
NEGC = 14.0
INV_SQRT_D = float(1.0 / np.sqrt(np.float32(128.0)))

ENGS = ("sync", "vector", "scalar", "tensor", "gpsimd")


class Sem:
    def __init__(self, name):
        self.name = name
        self.n = 0
        self.h = None


def build(S, n_dbg=0, STAG=2):
    NT = S // 128
    NSC = S // 512
    nc = bass.Bass("TRN2", target_bir_lowering=False, debug=False)

    d_x = nc.dram_tensor("x", [S, E], F32, kind="ExternalInput").ap()
    d_xT = nc.dram_tensor("xT", [E, S], F32, kind="ExternalInput").ap()
    d_wq = nc.dram_tensor("wq", [H, 128, NEC, 128], F32, kind="ExternalInput").ap()
    d_wk = nc.dram_tensor("wk", [KVH, 128, NEC, 128], F32, kind="ExternalInput").ap()
    d_wv = nc.dram_tensor("wv", [E, 512], F32, kind="ExternalInput").ap()
    d_wrq = nc.dram_tensor("wrq", [3, 128, NEC, 128], F32, kind="ExternalInput").ap()
    d_wrk = nc.dram_tensor("wrk", [3, 128, NEC, 128], F32, kind="ExternalInput").ap()
    d_wo = nc.dram_tensor("wo", [16, 128, NEC, 128], F32, kind="ExternalInput").ap()
    d_w0 = nc.dram_tensor("w0", [64, 128, NEC, 128], F32, kind="ExternalInput").ap()
    d_w1 = nc.dram_tensor("w1", [16, 128, 64, 128], F32, kind="ExternalInput").ap()
    d_qnw = nc.dram_tensor("qnw", [128, 2], F32, kind="ExternalInput").ap()
    d_knw = nc.dram_tensor("knw", [128, 2], F32, kind="ExternalInput").ap()
    d_cos = nc.dram_tensor("costab", [128, S], F32, kind="ExternalInput").ap()
    d_sin = nc.dram_tensor("sintab", [128, S], F32, kind="ExternalInput").ap()
    d_rotm = nc.dram_tensor("rotm", [128, 128], BF16, kind="ExternalInput").ap()
    d_ident = nc.dram_tensor("ident", [128, 128], BF16, kind="ExternalInput").ap()
    d_mask = nc.dram_tensor("maskbuf", [128, 896], BF16, kind="ExternalInput").ap()
    d_gs = nc.dram_tensor("gs2", [128, 3], F32, kind="ExternalInput").ap()
    d_b0 = nc.dram_tensor("b0r", [128, 64], F32, kind="ExternalInput").ap()
    d_b1 = nc.dram_tensor("b1r", [128, 16], F32, kind="ExternalInput").ap()
    d_cst = nc.dram_tensor("cst", [128, 8], F32, kind="ExternalInput").ap()
    d_onesc = nc.dram_tensor("onesc", [128, 1], BF16, kind="ExternalInput").ap()
    d_onesr = nc.dram_tensor("onesr", [1, 128], BF16, kind="ExternalInput").ap()
    d_onesm = nc.dram_tensor("onesm", [128, 128], BF16, kind="ExternalInput").ap()
    d_y1sp = nc.dram_tensor("y1sp", [E, S], BF16, kind="Internal").ap()
    d_wbf = nc.dram_tensor("wbf", [160, 128, NEC, 128], BF16, kind="Internal").ap()
    d_out = nc.dram_tensor("outT", [E, S], F32, kind="ExternalOutput").ap()

    C_NEGC, C_EPSLN, C_EPSRMS, C_INV_E, C_INV_D, C_HALF, C_ZERO, C_ONE = range(8)

    KT_O = 0
    V_O = KT_O + 4 * S
    RSQ_O = V_O + NT * 512
    RSK_O = RSQ_O + 3 * S
    QT_O = RSK_O + 3 * S
    AT_O = QT_O + 8 * 512
    PB_O = AT_O + 16 * 512
    AM_O = PB_O + 4 * 512
    TB_O = AM_O + 9 * 512
    HACT_O = 0
    W1S_O = 64 * 512
    Y1T_O = W1S_O + 4 * 2048
    ARENA = max(TB_O + 2 * 512, Y1T_O + 2 * 512)
    SQ_O = 0
    XN_O = 2048
    TK_O = QT_O

    AM_QSQ = AM_O
    AM_QRAW = AM_O + 512
    AM_T1 = AM_O + 2 * 512
    AM_RSBC = AM_O + 3 * 512
    AM_RQS = AM_O + 4 * 512
    AM_RKS = AM_O + 5 * 512
    AM_Y1 = AM_O + 6 * 512
    AM_RS2 = AM_O + 8 * 512

    prog = {e: [] for e in ENGS}
    sLD = Sem("sLD")
    sST = Sem("sST")
    sL2 = Sem("sL2")
    sPE = Sem("sPE")
    sAC = Sem("sAC")
    sDV = Sem("sDV")
    sGP = Sem("sGP")
    sems = [sLD, sST, sL2, sPE, sAC, sDV, sGP]

    def op(eng, fn, *waits):
        prog[eng].append((tuple(waits), fn))

    def dma(q, dst, src, *waits):
        sem = {"sync": sLD, "gpsimd": sST, "scalar": sL2}[q]
        def fn(e, dst=dst, src=src, sem=sem):
            e.dma_start(dst, src).then_inc(sem.h, 16)
        op(q, fn, (sem, sem.n), *waits)
        sem.n += 16
        return sem.n

    def pe(fn, *waits, inc=False):
        if inc:
            def fn2(e, fn=fn):
                fn(e).then_inc(sPE.h, 1)
            op("tensor", fn2, *waits)
            sPE.n += 1
            return sPE.n
        op("tensor", fn, *waits)
        return sPE.n

    def flush(eng, *waits):
        sem = {"vector": sDV, "scalar": sAC, "gpsimd": sGP}[eng]
        def fn(e, sem=sem):
            e.drain().then_inc(sem.h, 1)
        op(eng, fn, *waits)
        sem.n += 1
        return sem.n

    import contextlib
    cx = contextlib.ExitStack()
    EC = cx.enter_context
    xnT = EC(nc.sbuf_tensor("xnT", [128, NEC * S], BF16))
    ar = EC(nc.sbuf_tensor("ar", [128, ARENA], BF16))
    wcbf = EC(nc.sbuf_tensor("wcbf", [128, 2, 2048], F32))
    wcbb = EC(nc.sbuf_tensor("wcbb", [128, 2 * 2048], BF16))
    fst = EC(nc.sbuf_tensor("fst", [128, 3 * 512], F32))
    xtb = EC(nc.sbuf_tensor("xtb", [128, 2 * 512], F32))
    rowf = EC(nc.sbuf_tensor("rowf", [128, 1024], F32))
    rbf = EC(nc.sbuf_tensor("rbf", [1, 512], BF16))
    cstS = EC(nc.sbuf_tensor("cstS", [128, 8], F32))
    qnwS = EC(nc.sbuf_tensor("qnwS", [128, 2], F32))
    knwS = EC(nc.sbuf_tensor("knwS", [128, 2], F32))
    gsS = EC(nc.sbuf_tensor("gsS", [128, 4], F32))
    b0S = EC(nc.sbuf_tensor("b0S", [128, 64], F32))
    b1S = EC(nc.sbuf_tensor("b1S", [128, 16], F32))
    lnS = EC(nc.sbuf_tensor("lnS", [128, 8], F32))
    rotmS = EC(nc.sbuf_tensor("rotmS", [128, 128], BF16))
    identS = EC(nc.sbuf_tensor("identS", [128, 128], BF16))
    maskS = EC(nc.sbuf_tensor("maskS", [128, 896], BF16))
    onescS = EC(nc.sbuf_tensor("onescS", [128, 1], BF16))
    onesrS = EC(nc.sbuf_tensor("onesrS", [1, 128], BF16))
    onesmS = EC(nc.sbuf_tensor("onesmS", [128, 128], BF16))
    ps = [EC(nc.psum_tensor("ps%d" % i, [128, 512], F32)) for i in range(8)]

    wstate = {"wfi": 0}
    wslot_free = [(sDV, 0), (sDV, 0)]
    WSP = {}

    def feed32(src_ap, width=2048, q="sync", *extra):
        sl = wstate["wfi"] % 2
        wstate["wfi"] += 1
        v = dma(q, wcbf[:, sl, 0:width], src_ap, wslot_free[sl], *extra)
        return sl, v

    v0 = dma("sync", cstS[:, :], d_cst[:, :])
    dma("sync", qnwS[:, :], d_qnw[:, :])
    dma("sync", knwS[:, :], d_knw[:, :])
    dma("sync", gsS[:, 0:3], d_gs[:, :])
    dma("sync", b0S[:, :], d_b0[:, :])
    dma("sync", b1S[:, :], d_b1[:, :])
    dma("sync", rotmS[:, :], d_rotm[:, :])
    dma("sync", identS[:, :], d_ident[:, :])
    dma("sync", maskS[:, :], d_mask[:, :])
    dma("sync", onescS[:, :], d_onesc[:, :])
    dma("sync", onesrS[:, :], d_onesr[:, :])
    v_const = dma("sync", onesmS[:, :], d_onesm[:, :])
    def f_gshalf(e):
        e.tensor_scalar(gsS[:, 3:4], gsS[:, 0:1], 0.5, None, ALU.mult)
    op("vector", f_gshalf, (sLD, v_const))
    v_gs = flush("vector")

    xn_done_pe = 0
    last_norm_dv = 0
    for st in range(NT):
        xsl, v_x = feed32(d_x[st * 128:(st + 1) * 128, :])
        def f_red(e, xsl=xsl):
            e.reduce_sum(lnS[:, 0:1], wcbf[:, xsl, :], axis=AX.X)
        op("vector", f_red, (sLD, v_x))
        v_red = flush("vector")
        def f_sq(e, xsl=xsl):
            e.activation(ar[:, SQ_O:SQ_O + 2048], wcbf[:, xsl, :], AF.Square,
                         accum_out=lnS[:, 1:2])
        op("scalar", f_sq, (sLD, v_x))
        v_sq = flush("scalar")
        def f_ln1(e):
            e.tensor_scalar(lnS[:, 2:3], lnS[:, 0:1], 1.0 / 2048.0, None, ALU.mult)
            e.tensor_scalar(lnS[:, 3:4], lnS[:, 1:2], 1.0 / 2048.0,
                            cstS[:, C_EPSLN:C_EPSLN + 1], ALU.mult, ALU.add)
        op("vector", f_ln1, (sDV, v_red), (sAC, v_sq))
        flush("vector")
        def f_ln2(e):
            e.scalar_tensor_tensor(lnS[:, 4:5], lnS[:, 2:3], -1.0, lnS[:, 2:3],
                                   ALU.mult, ALU.mult)
        op("vector", f_ln2)
        flush("vector")
        def f_ln3(e):
            e.tensor_tensor(lnS[:, 5:6], lnS[:, 3:4], lnS[:, 4:5], ALU.add)
        op("vector", f_ln3)
        flush("vector")
        def f_ln4(e):
            e.reciprocal(lnS[:, 6:7], lnS[:, 5:6])
        op("vector", f_ln4)
        v_rec = flush("vector")
        def f_ln5(e):
            e.activation(lnS[:, 7:8], lnS[:, 6:7], AF.Sqrt)
        op("scalar", f_ln5, (sDV, v_rec))
        v_rs = flush("scalar")
        def f_ln6(e):
            e.scalar_tensor_tensor(lnS[:, 4:5], lnS[:, 2:3], -1.0, lnS[:, 7:8],
                                   ALU.mult, ALU.mult)
        op("vector", f_ln6, (sAC, v_rs))
        flush("vector")
        xslot = XN_O + (st % 2) * 2048
        def f_norm(e, xslot=xslot, xsl=xsl):
            e.tensor_scalar(ar[:, xslot:xslot + 2048], wcbf[:, xsl, :],
                            lnS[:, 7:8], lnS[:, 4:5], ALU.mult, ALU.add)
        op("vector", f_norm, (sPE, xn_done_pe), (sAC, v_sq))
        last_norm_dv = flush("vector")
        wslot_free[xsl] = (sDV, last_norm_dv)
        evac_dv = {}
        for g4 in range(4):
            bank = ps[6 + (g4 % 2)]
            w_ev = evac_dv.get(g4 - 2, 0)
            for k in range(4):
                ecx = g4 * 4 + k
                def f_tr(e, bank=bank, xslot=xslot, ecx=ecx, k=k):
                    return e.matmul(bank[:, k * 128:(k + 1) * 128],
                                    ar[:, xslot + ecx * 128:xslot + (ecx + 1) * 128],
                                    identS[:, :], start=True, stop=True)
                v_pe = pe(f_tr, (sDV, last_norm_dv), (sDV, w_ev),
                          inc=(k == 3))
            def f_ev(e, bank=bank, g4=g4, st=st):
                for k in range(4):
                    ecx = g4 * 4 + k
                    e.tensor_copy(xnT[:, ecx * S + st * 128: ecx * S + (st + 1) * 128],
                                  bank[:, k * 128:(k + 1) * 128])
            op("vector", f_ev, (sPE, v_pe))
            evac_dv[g4] = flush("vector")
        xn_done_pe = sPE.n

    v_xnT_dv = sDV.n

    csl, v_c = feed32(d_cos[:, :], S)
    def f_foldkc(e, csl=csl):
        e.tensor_scalar(ar[:, TK_O:TK_O + S], wcbf[:, csl, 0:S],
                        knwS[:, 0:1], None, ALU.mult)
    op("vector", f_foldkc, (sLD, v_c))
    v_fkc = flush("vector")
    wslot_free[csl] = (sDV, v_fkc)
    ssl, v_s = feed32(d_sin[:, :], S)
    def f_foldks(e, ssl=ssl):
        e.tensor_scalar(ar[:, TK_O + S:TK_O + 2 * S], wcbf[:, ssl, 0:S],
                        knwS[:, 1:2], None, ALU.mult)
    op("vector", f_foldks, (sLD, v_s))
    v_fks = flush("vector")
    wslot_free[ssl] = (sDV, v_fks)

    ps2_free = (sDV, 0)
    wcbb_free = [((sPE, 0),), ((sPE, 0),)]
    psum_free = [(sDV, 0), (sDV, 0)]
    cb_list = [("k", g) for g in range(KVH)] + \
              [("rq", rc) for rc in range(3)] + [("rk", rc) for rc in range(3)]
    it = 0
    for kind, idx in cb_list:
        src = {"k": d_wk, "rq": d_wrq, "rk": d_wrk}[kind]
        fsl, v_w = feed32(src[idx, :, :, :])
        slot = it % 2
        def f_cast(e, slot=slot, fsl=fsl):
            e.tensor_copy(wcbb[:, slot * 2048:(slot + 1) * 2048], wcbf[:, fsl, :])
        op("vector", f_cast, (sLD, v_w), *wcbb_free[slot])
        v_cast = flush("vector")
        wslot_free[fsl] = (sDV, v_cast)
        for tc in range(NSC):
            par = (it * NSC + tc) % 2
            bank = ps[par]
            for ec in range(NEC):
                def f_mm(e, bank=bank, slot=slot, ec=ec, tc=tc):
                    return e.matmul(bank[:, :],
                                    wcbb[:, slot * 2048 + ec * 128:
                                         slot * 2048 + (ec + 1) * 128],
                                    xnT[:, ec * S + tc * 512: ec * S + tc * 512 + 512],
                                    start=(ec == 0), stop=(ec == NEC - 1))
                v_mm = pe(f_mm, (sDV, v_cast), psum_free[par],
                          (sDV, v_xnT_dv) if ec == 0 and it == 0 else (sDV, 0),
                          inc=(ec == NEC - 1))
            if kind == "k":
                g, tcp = idx, tc
                def f_ksq(e, bank=bank):
                    e.activation(ar[:, AM_QSQ:AM_QSQ + 512], bank[:, :], AF.Square)
                op("scalar", f_ksq, (sPE, v_mm))
                v_ksq = flush("scalar")
                def f_dmm(e):
                    return e.matmul(ps[2][:, :], onesmS[:, :],
                                    ar[:, AM_QSQ:AM_QSQ + 512], start=True, stop=True)
                v_dmm = pe(f_dmm, (sAC, v_ksq), ps2_free, inc=True)
                def f_r1(e):
                    e.activation(rowf[:, 0:512], ps[2][:, :], AF.Sqrt,
                                 bias=cstS[:, C_EPSRMS:C_EPSRMS + 1],
                                 scale=cstS[:, C_INV_D:C_INV_D + 1])
                op("scalar", f_r1, (sPE, v_dmm))
                ps2_free = (sAC, flush("scalar"))
                def f_r3(e):
                    e.reciprocal(rowf[:, 512:1024], rowf[:, 0:512])
                op("vector", f_r3, ps2_free)
                v_r3 = flush("vector")
                def f_kraw(e, bank=bank):
                    e.tensor_tensor(ar[:, AM_QRAW:AM_QRAW + 512], bank[:, :],
                                    rowf[:, 512:1024], ALU.mult)
                op("vector", f_kraw, (sDV, v_r3))
                v_kraw = flush("vector")
                psum_free[par] = (sDV, v_kraw)
                def f_rot(e):
                    return e.matmul(ps[3][:, :], rotmS[:, :],
                                    ar[:, AM_QRAW:AM_QRAW + 512], start=True, stop=True)
                v_rot = pe(f_rot, (sDV, v_kraw), inc=True)
                def f_t1(e, tcp=tcp):
                    e.tensor_tensor(ar[:, AM_T1:AM_T1 + 512],
                                    ar[:, AM_QRAW:AM_QRAW + 512],
                                    ar[:, TK_O + tcp * 512:TK_O + tcp * 512 + 512],
                                    ALU.mult)
                op("vector", f_t1)
                flush("vector")
                def f_t2(e, tcp=tcp):
                    e.tensor_tensor(ar[:, AM_QRAW:AM_QRAW + 512], ps[3][:, :],
                                    ar[:, TK_O + S + tcp * 512:TK_O + S + tcp * 512 + 512],
                                    ALU.mult)
                op("vector", f_t2, (sPE, v_rot))
                flush("vector")
                def f_kT(e, g=g, tcp=tcp):
                    e.tensor_tensor(ar[:, KT_O + g * S + tcp * 512:
                                       KT_O + g * S + tcp * 512 + 512],
                                    ar[:, AM_T1:AM_T1 + 512],
                                    ar[:, AM_QRAW:AM_QRAW + 512], ALU.add)
                op("vector", f_kT)
                flush("vector")
            else:
                dst_o = (RSQ_O if kind == "rq" else RSK_O) + idx * S + tc * 512
                def f_ev(e, bank=bank, dst_o=dst_o):
                    e.tensor_copy(ar[:, dst_o:dst_o + 512], bank[:, :])
                op("vector", f_ev, (sPE, v_mm))
                psum_free[par] = (sDV, flush("vector"))
        wcbb_free[slot] = ((sPE, sPE.n),)
        it += 1

    vps_free = [(sAC, 0), (sAC, 0), (sDV, v_xnT_dv), (sDV, v_xnT_dv)]
    for ttg in range(NT // 4):
        for ec in range(NEC):
            fsl, v_w = feed32(d_wv[ec * 128:(ec + 1) * 128, :], 512)
            slot = ec % 2
            def f_cast(e, slot=slot, fsl=fsl):
                e.tensor_copy(wcbb[:, slot * 2048:slot * 2048 + 512],
                              wcbf[:, fsl, 0:512])
            op("vector", f_cast, (sLD, v_w), *wcbb_free[slot])
            v_cast = flush("vector")
            wslot_free[fsl] = (sDV, v_cast)
            for j in range(4):
                tt = ttg * 4 + j
                def f_mm(e, j=j, slot=slot, ec=ec, tt=tt):
                    return e.matmul(ps[4 + j][:, :],
                                    xnT[:, ec * S + tt * 128: ec * S + (tt + 1) * 128],
                                    wcbb[:, slot * 2048:slot * 2048 + 512],
                                    start=(ec == 0), stop=(ec == NEC - 1))
                v_mm = pe(f_mm, (sDV, v_cast), vps_free[j],
                          inc=(j == 3 or ec == NEC - 1))
            wcbb_free[slot] = ((sPE, sPE.n),)
        for j in range(4):
            tt = ttg * 4 + j
            def f_ev(e, j=j, tt=tt):
                e.activation(ar[:, V_O + tt * 512:V_O + (tt + 1) * 512],
                             ps[4 + j][:, :], AF.Copy)
            op("scalar", f_ev, (sPE, v_mm))
            vps_free[j] = (sAC, flush("scalar"))

    a0_end_pe = sPE.n
    a0_end_dv = sDV.n
    a0_end_ac = sAC.n

    y1_spills = []
    wcbb_free_m = wcbb_free
    tab_dv = 0
    pslot_free = [(sPE, 0)] * 4
    qps_free = [(sAC, a0_end_ac), (sAC, a0_end_ac)]
    sc_ps_free = [(sDV, a0_end_dv), (sDV, a0_end_dv)]
    av_free = [(sAC, a0_end_ac), (sAC, a0_end_ac)]
    qT_free = [(sPE, 0)] * 8
    y1st_free = [(sST, 0), (sST, 0)]
    rqs_free = (sPE, 0)
    ps3_free = (sDV, 0)
    pend_end = None
    rope_q = []
    rdn_free = [(sDV, 0)]
    rs_free = [(sDV, 0), (sDV, 0), (sDV, 0)]
    sqr_free = [(sDV, 0), (sDV, 0)]
    spill_q = []
    for j in range(64):
        spill_q.append((j, lambda j=j: d_w0[j, :, :, :]))
    for eoc in range(16):
        for qq in range(4):
            spill_q.append((64 + eoc * 4 + qq,
                            lambda eoc=eoc, qq=qq: d_w1[eoc, :, qq * 16:(qq + 1) * 16, :]))
    spill_i = [0]
    sb_slot = [0]

    def emit_spills(k):
        for _ in range(k):
            if spill_i[0] >= len(spill_q):
                return
            blk, srcfn = spill_q[spill_i[0]]
            spill_i[0] += 1
            fsl, v_w = feed32(srcfn(), 2048, "scalar")
            bsl = sb_slot[0] % 2
            sb_slot[0] += 1
            def f_cast(e, bsl=bsl, fsl=fsl):
                e.tensor_copy(wcbb[:, bsl * 2048:(bsl + 1) * 2048], wcbf[:, fsl, :])
            op("vector", f_cast, (sL2, v_w), *wcbb_free_m[bsl])
            v_cast = flush("vector")
            wslot_free[fsl] = (sDV, v_cast)
            WSP[blk] = dma("gpsimd", d_wbf[blk, :, :, :],
                           wcbb[:, bsl * 2048:(bsl + 1) * 2048], (sDV, v_cast))
            wcbb_free_m[bsl] = ((sST, WSP[blk]),)
    aT_read_pe = 0
    for sc in range(NSC):
        nt = 4 * (sc + 1)
        v_c = dma("sync", fst[:, 0:512], d_cos[:, sc * 512:(sc + 1) * 512],
                  (sDV, sDV.n))
        def f_fqc(e):
            e.tensor_scalar(ar[:, TB_O:TB_O + 512], fst[:, 0:512],
                            qnwS[:, 0:1], INV_SQRT_D, ALU.mult, ALU.mult)
        op("vector", f_fqc, (sLD, v_c))
        v_fq = flush("vector")
        v_s2 = dma("sync", fst[:, 0:512], d_sin[:, sc * 512:(sc + 1) * 512],
                   (sDV, v_fq))
        def f_fqs(e):
            e.tensor_scalar(ar[:, TB_O + 512:TB_O + 1024], fst[:, 0:512],
                            qnwS[:, 1:2], INV_SQRT_D, ALU.mult, ALU.mult)
        op("vector", f_fqs, (sLD, v_s2))
        tab_dv = flush("vector")

        for half in range(2):
            for hh in range(8):
                h = half * 8 + hh
                slot = h % 2
                if sc == 0:
                    fsl, v_w = feed32(d_wq[h, :, :, :])
                    def f_cast(e, slot=slot, fsl=fsl):
                        e.tensor_copy(wcbb[:, slot * 2048:(slot + 1) * 2048],
                                      wcbf[:, fsl, :])
                    op("vector", f_cast, (sLD, v_w), *wcbb_free_m[slot])
                    v_cast = flush("vector")
                    wslot_free[fsl] = (sDV, v_cast)
                    WSP[128 + h] = dma("gpsimd", d_wbf[128 + h, :, :, :],
                                       wcbb[:, slot * 2048:(slot + 1) * 2048],
                                       (sDV, v_cast))
                    mm_w = (sDV, v_cast)
                else:
                    v_w = dma("sync", wcbb[:, slot * 2048:(slot + 1) * 2048],
                              d_wbf[128 + h, :, :, :], *wcbb_free_m[slot],
                              (sST, WSP[128 + h]))
                    mm_w = (sLD, v_w)
                par = h % 2
                bank = ps[6 + par]
                for ec in range(NEC):
                    def f_mm(e, bank=bank, slot=slot, ec=ec, sc=sc):
                        return e.matmul(bank[:, :],
                                        wcbb[:, slot * 2048 + ec * 128:
                                             slot * 2048 + (ec + 1) * 128],
                                        xnT[:, ec * S + sc * 512: ec * S + sc * 512 + 512],
                                        start=(ec == 0), stop=(ec == NEC - 1))
                    v_mm = pe(f_mm, mm_w, qps_free[par],
                              inc=(ec == NEC - 1))
                wcbb_free_m[slot] = (((sPE, sPE.n),) if sc > 0 else
                                     ((sPE, sPE.n), (sST, WSP[128 + h])))
                def f_qsq(e, bank=bank):
                    e.activation(ar[:, AM_QSQ:AM_QSQ + 512], bank[:, :], AF.Square)
                op("scalar", f_qsq, (sPE, v_mm), (sDV, tab_dv))
                v_qsq = flush("scalar")
                def f_dmm(e):
                    return e.matmul(ps[2][:, :], onesmS[:, :],
                                    ar[:, AM_QSQ:AM_QSQ + 512], start=True, stop=True)
                v_dmm = pe(f_dmm, (sAC, v_qsq), ps2_free, inc=True)
                sqslot = rowf[:, 0:512] if hh % 2 == 0 else xtb[:, 0:512]
                def f_r1(e, sqslot=sqslot):
                    e.activation(sqslot, ps[2][:, :], AF.Sqrt,
                                 bias=cstS[:, C_EPSRMS:C_EPSRMS + 1],
                                 scale=cstS[:, C_INV_D:C_INV_D + 1])
                op("scalar", f_r1, (sPE, v_dmm), sqr_free[hh % 2])
                ps2_free = (sAC, flush("scalar"))
                rs_o = (hh % 3) * 512
                def f_r3(e, rs_o=rs_o, sqslot=sqslot):
                    e.reciprocal(fst[:, rs_o:rs_o + 512], sqslot)
                op("vector", f_r3, ps2_free, rs_free[hh % 3])
                v_r3 = flush("vector")
                sqr_free[hh % 2] = (sDV, v_r3)

                def mk_rope(hh=hh, h=h, bank=bank, par=par, rs_o=rs_o, v_r3=v_r3):
                    def doit():
                        nonlocal qps_free
                        def f_qraw(e, bank=bank, rs_o=rs_o):
                            e.tensor_tensor(ar[:, AM_QRAW:AM_QRAW + 512], bank[:, :],
                                            fst[:, rs_o:rs_o + 512], ALU.mult)
                        op("vector", f_qraw, (sDV, v_r3))
                        rs_free[rs_o // 512] = (sDV, sDV.n + 1)
                        v_qraw = flush("vector")
                        qps_free[par] = (sDV, v_qraw)
                        def f_rot(e):
                            return e.matmul(ps[3][:, :], rotmS[:, :],
                                            ar[:, AM_QRAW:AM_QRAW + 512],
                                            start=True, stop=True)
                        v_rot = pe(f_rot, (sDV, v_qraw), inc=True)
                        def f_t1(e):
                            e.tensor_tensor(ar[:, AM_T1:AM_T1 + 512],
                                            ar[:, AM_QRAW:AM_QRAW + 512],
                                            ar[:, TB_O:TB_O + 512], ALU.mult)
                        op("vector", f_t1)
                        flush("vector")
                        def f_t2(e):
                            e.tensor_tensor(ar[:, AM_QRAW:AM_QRAW + 512], ps[3][:, :],
                                            ar[:, TB_O + 512:TB_O + 1024], ALU.mult)
                        op("vector", f_t2, (sPE, v_rot))
                        flush("vector")
                        def f_qT(e, hh=hh):
                            e.tensor_tensor(ar[:, QT_O + hh * 512:QT_O + hh * 512 + 512],
                                            ar[:, AM_T1:AM_T1 + 512],
                                            ar[:, AM_QRAW:AM_QRAW + 512], ALU.add)
                        op("vector", f_qT, qT_free[hh])
                        qT_free[hh] = (sDV, flush("vector"))
                    return doit
                rope_q.append(mk_rope())
                if len(rope_q) > 1:
                    rope_q.pop(0)()
            while rope_q:
                rope_q.pop(0)()

            for hh in range(8):
                h = half * 8 + hh
                g = h // 4
                jo = 32 * (h % 4)
                v_rqs = 0
                if h < HG:
                    gcol = 3 if h < SYM else h // 4
                    def f_rqs(e, h=h, jo=jo, gcol=gcol, sc=sc):
                        e.tensor_scalar(ar[jo:jo + 32, AM_RQS:AM_RQS + 512],
                                        ar[jo:jo + 32, RSQ_O + (h // 4) * S + sc * 512:
                                           RSQ_O + (h // 4) * S + sc * 512 + 512],
                                        gsS[jo:jo + 32, gcol:gcol + 1], None, ALU.mult)
                        if h < SYM:
                            e.tensor_scalar(ar[jo:jo + 32, AM_RKS:AM_RKS + 512],
                                            ar[jo:jo + 32, RSK_O + (h // 4) * S + sc * 512:
                                               RSK_O + (h // 4) * S + sc * 512 + 512],
                                            gsS[jo:jo + 32, gcol:gcol + 1], None, ALU.mult)
                    op("vector", f_rqs, rqs_free, (sDV, v_gs), qT_free[hh])
                    v_rqs = flush("vector")
                av_q = []
                dn = {}
                for tt in range(nt):
                    par = tt % 2
                    bank = ps[par]
                    def f_qk(e, bank=bank, g=g, tt=tt, hh=hh, h=h):
                        return e.matmul(bank[:, :],
                                        ar[:, KT_O + g * S + tt * 128:
                                           KT_O + g * S + (tt + 1) * 128],
                                        ar[:, QT_O + hh * 512:QT_O + hh * 512 + 512],
                                        start=True, stop=(h >= HG))
                    v_sc = pe(f_qk, sc_ps_free[par], qT_free[hh],
                              (sDV, v_rqs) if h < HG else (sDV, 0),
                              inc=(h >= HG))
                    if h < HG:
                        issym = h < SYM
                        def f_b1m(e, bank=bank, h=h, jo=jo, tt=tt, issym=issym):
                            return e.matmul(bank[:, :],
                                            ar[jo:jo + 32, RSK_O + (h // 4) * S + tt * 128:
                                               RSK_O + (h // 4) * S + (tt + 1) * 128],
                                            ar[jo:jo + 32, AM_RQS:AM_RQS + 512],
                                            start=False, stop=(not issym),
                                            tile_position=(jo, 0))
                        v_sc = pe(f_b1m, inc=(not issym))
                        if issym:
                            def f_b2m(e, bank=bank, h=h, jo=jo, tt=tt):
                                return e.matmul(bank[:, :],
                                                ar[jo:jo + 32, RSQ_O + (h // 4) * S + tt * 128:
                                                   RSQ_O + (h // 4) * S + (tt + 1) * 128],
                                                ar[jo:jo + 32, AM_RKS:AM_RKS + 512],
                                                start=False, stop=True,
                                                tile_position=(jo, 0))
                            v_sc = pe(f_b2m, inc=True)
                    slot = tt % 4
                    def f_exp(e, bank=bank, slot=slot):
                        e.activation(ar[:, PB_O + slot * 512:PB_O + (slot + 1) * 512],
                                     bank[:, :], AF.Exp, bias=cstS[:, C_NEGC:C_NEGC + 1])
                    op("scalar", f_exp, (sPE, v_sc), pslot_free[slot])
                    v_exp = flush("scalar")
                    sc_ps_free[par] = (sAC, v_exp)
                    p_ready = (sAC, v_exp)
                    if tt >= 4 * sc:
                        off = 128 * tt - 512 * sc
                        def f_msk(e, slot=slot, off=off):
                            e.tensor_tensor(ar[:, PB_O + slot * 512:PB_O + (slot + 1) * 512],
                                            ar[:, PB_O + slot * 512:PB_O + (slot + 1) * 512],
                                            maskS[:, 384 - off:384 - off + 512], ALU.mult)
                        op("gpsimd", f_msk, (sAC, v_exp))
                        p_ready = (sGP, flush("gpsimd"))
                    avb = ps[4 + (h % 2)]
                    def mk_avden(tt=tt, slot=slot, p_ready=p_ready, nt=nt,
                                 avb=avb, g=g, h=h):
                        def doit():
                            def f_av(e, avb=avb, tt=tt, g=g, slot=slot, nt=nt):
                                return e.matmul(avb[:, :],
                                                ar[:, V_O + tt * 512 + g * 128:
                                                   V_O + tt * 512 + (g + 1) * 128],
                                                ar[:, PB_O + slot * 512:PB_O + (slot + 1) * 512],
                                                start=(tt == 0), stop=(tt == nt - 1))
                            pe(f_av, p_ready, av_free[h % 2] if tt == 0 else (sPE, 0))
                            def f_den(e, tt=tt, slot=slot, nt=nt):
                                return e.matmul(ps[2][:, :], onesmS[:, :],
                                                ar[:, PB_O + slot * 512:PB_O + (slot + 1) * 512],
                                                start=(tt == 0), stop=(tt == nt - 1))
                            v = pe(f_den, ps2_free if tt == 0 else (sPE, 0), inc=True)
                            pslot_free[slot] = (sPE, v)
                            dn["v"] = v
                        return doit
                    av_q.append(mk_avden())
                    if tt >= 2:
                        av_q[tt - 2]()
                if nt >= 2:
                    av_q[nt - 2]()
                av_q[nt - 1]()
                v_avden = dn["v"]
                def f_rd1(e):
                    e.reciprocal(rowf[:, 512:1024], ps[2][:, :])
                op("vector", f_rd1, (sPE, v_avden), rdn_free[0])
                ps2_free = (sDV, flush("vector"))
                def f_aT(e, h=h, avb=avb):
                    e.tensor_tensor(ar[:, AT_O + h * 512:AT_O + h * 512 + 512],
                                    avb[:, :], rowf[:, 512:1024], ALU.mult)
                op("vector", f_aT, ps2_free)
                v_aTf = flush("vector")
                av_free[h % 2] = (sDV, v_aTf)
                rdn_free[0] = (sDV, v_aTf)
                rqs_free = (sPE, sPE.n)
                emit_spills([1, 2, 3, 3][sc] if S == 2048 else 8)
        v_aT_all = sDV.n

        for eoc in range(16):
            slot = eoc % 2
            if sc == 0:
                fsl, v_w = feed32(d_wo[eoc, :, :, :])
                def f_cast(e, slot=slot, fsl=fsl):
                    e.tensor_copy(wcbb[:, slot * 2048:(slot + 1) * 2048],
                                  wcbf[:, fsl, :])
                op("vector", f_cast, (sLD, v_w), *wcbb_free_m[slot])
                v_cast = flush("vector")
                wslot_free[fsl] = (sDV, v_cast)
                WSP[144 + eoc] = dma("gpsimd", d_wbf[144 + eoc, :, :, :],
                                     wcbb[:, slot * 2048:(slot + 1) * 2048],
                                     (sDV, v_cast))
                mm_w = (sDV, v_cast)
            else:
                v_w = dma("sync", wcbb[:, slot * 2048:(slot + 1) * 2048],
                          d_wbf[144 + eoc, :, :, :], *wcbb_free_m[slot],
                          (sST, WSP[144 + eoc]))
                mm_w = (sLD, v_w)
            par = eoc % 2
            bank = ps[6 + par]
            for ec in range(NEC):
                def f_mm(e, bank=bank, slot=slot, ec=ec):
                    return e.matmul(bank[:, :],
                                    wcbb[:, slot * 2048 + ec * 128:
                                         slot * 2048 + (ec + 1) * 128],
                                    ar[:, AT_O + ec * 512:AT_O + (ec + 1) * 512],
                                    start=(ec == 0), stop=(ec == NEC - 1))
                v_mm = pe(f_mm, mm_w, qps_free[par], (sDV, v_aT_all),
                          inc=(ec == NEC - 1))
            wcbb_free_m[slot] = (((sPE, sPE.n),) if sc > 0 else
                                 ((sPE, sPE.n), (sST, WSP[144 + eoc])))
            y1slot = AM_Y1 + (eoc % 2) * 512
            def f_ev(e, bank=bank, y1slot=y1slot):
                e.activation(ar[:, y1slot:y1slot + 512], bank[:, :], AF.Copy)
            op("scalar", f_ev, (sPE, v_mm), y1st_free[eoc % 2])
            v_ev = flush("scalar")
            qps_free[par] = (sAC, v_ev)
            v_sp = dma("gpsimd", d_y1sp[eoc * 128:(eoc + 1) * 128, sc * 512:sc * 512 + 512],
                       ar[:, y1slot:y1slot + 512], (sAC, v_ev))
            y1st_free[eoc % 2] = (sST, v_sp)
        aT_read_pe = sPE.n
    a_end_pe = sPE.n
    a_end_dv = sDV.n
    a_end_ac = sAC.n

    wcbb_free_b = wcbb_free_m
    y1sp_done = sST.n
    hact_free_pe = a_end_pe
    w0ps_free = [(sAC, a_end_ac), (sAC, a_end_ac)]
    w1ps_free = [(sDV, a_end_dv), (sDV, a_end_dv)]
    xtb_free = [(sDV, a_end_dv), (sDV, a_end_dv)]
    y1t_free = [(sDV, 0), (sDV, 0)]
    outst_free = [(sST, 0), (sST, 0)]
    for sc in range(NSC):
        for j in range(64):
            slot = j % 2
            v_w = dma("sync", wcbb[:, slot * 2048:(slot + 1) * 2048],
                      d_wbf[j, :, :, :], *wcbb_free_b[slot], (sST, WSP[j]),
                      (sPE, a_end_pe))
            par = j % 2
            bank = ps[par]
            for ec in range(NEC):
                def f_mm(e, bank=bank, slot=slot, ec=ec, sc=sc):
                    return e.matmul(bank[:, :],
                                    wcbb[:, slot * 2048 + ec * 128:
                                         slot * 2048 + (ec + 1) * 128],
                                    xnT[:, ec * S + sc * 512: ec * S + sc * 512 + 512],
                                    start=(ec == 0), stop=(ec == NEC - 1))
                v_mm = pe(f_mm, (sLD, v_w), w0ps_free[par], inc=(ec == NEC - 1))
            wcbb_free_b[slot] = ((sPE, sPE.n),)
            def f_gelu(e, bank=bank, j=j):
                e.activation(ar[:, HACT_O + j * 512:HACT_O + (j + 1) * 512],
                             bank[:, :], AF.Gelu, bias=b0S[:, j:j + 1])
            op("scalar", f_gelu, (sPE, v_mm), (sPE, hact_free_pe),
               (sST, y1sp_done) if sc == 0 and j == 0 else (sST, 0))
            w0ps_free[par] = (sAC, flush("scalar"))
        v_hact_ac = sAC.n

        for eoc in range(16):
            par = eoc % 2
            bank = ps[2 + par]
            for q in range(4):
                blk = 64 + eoc * 4 + q
                slot = (eoc * 4 + q) % 2
                v_w = dma("sync", wcbb[:, slot * 2048:(slot + 1) * 2048],
                          d_wbf[blk, :, :, :], *wcbb_free_b[slot],
                          (sST, WSP[blk]))
                for i in range(16):
                    ncb = q * 16 + i
                    def f_mm(e, bank=bank, slot=slot, i=i, ncb=ncb):
                        return e.matmul(bank[:, :],
                                        wcbb[:, slot * 2048 + i * 128:
                                             slot * 2048 + (i + 1) * 128],
                                        ar[:, HACT_O + ncb * 512:HACT_O + (ncb + 1) * 512],
                                        start=(ncb == 0), stop=(ncb == 63))
                    v_mm = pe(f_mm, (sLD, v_w), (sAC, v_hact_ac),
                              w1ps_free[par] if ncb == 0 else (sPE, 0),
                              inc=(i == 15 or ncb == 63))
                wcbb_free_b[slot] = ((sPE, sPE.n),)
            v_xt = dma("scalar", xtb[:, (eoc % 2) * 512:(eoc % 2) * 512 + 512],
                       d_xT[eoc * 128:(eoc + 1) * 128, sc * 512:sc * 512 + 512],
                       xtb_free[eoc % 2])
            y1o = Y1T_O + (eoc % 2) * 512
            v_y1 = dma("scalar", ar[:, y1o:y1o + 512],
                       d_y1sp[eoc * 128:(eoc + 1) * 128, sc * 512:sc * 512 + 512],
                       y1t_free[eoc % 2], (sPE, a_end_pe), (sST, y1sp_done))
            def f_f1(e, bank=bank, eoc=eoc, y1o=y1o):
                e.scalar_tensor_tensor(fst[:, 0:512], bank[:, :],
                                       b1S[:, eoc:eoc + 1], ar[:, y1o:y1o + 512],
                                       ALU.add, ALU.add)
            op("vector", f_f1, (sPE, v_mm), (sL2, v_y1))
            v_f1 = flush("vector")
            w1ps_free[par] = (sDV, v_f1)
            y1t_free[eoc % 2] = (sDV, v_f1)
            oslot = 512 + (eoc % 2) * 512
            def f_f2(e, eoc=eoc, oslot=oslot):
                e.tensor_tensor(fst[:, oslot:oslot + 512], fst[:, 0:512],
                                xtb[:, (eoc % 2) * 512:(eoc % 2) * 512 + 512], ALU.add)
            op("vector", f_f2, (sL2, v_xt), outst_free[eoc % 2])
            v_f2 = flush("vector")
            xtb_free[eoc % 2] = (sDV, v_f2)
            v_out = dma("gpsimd", d_out[eoc * 128:(eoc + 1) * 128, sc * 512:sc * 512 + 512],
                        fst[:, oslot:oslot + 512], (sDV, v_f2))
            outst_free[eoc % 2] = (sST, v_out)
        hact_free_pe = sPE.n

    def f_fin(e, v=sST.n):
        e.wait_ge(sST.h, v)
    prog["gpsimd"].append(((), f_fin))

    with cx:
        for s_ in sems:
            s_.h = cx.enter_context(nc.semaphore(s_.name))
        block = cx.enter_context(nc.Block())

        def run_stream(eng_name):
            def runner(e):
                hw = {}
                for waits, fn in prog[eng_name]:
                    for sem, val in waits:
                        if val > 0 and hw.get(sem.name, 0) < val:
                            e.wait_ge(sem.h, val)
                            hw[sem.name] = val
                    fn(e)
            return runner
        block.sync(run_stream("sync"))
        block.vector(run_stream("vector"))
        block.scalar(run_stream("scalar"))
        block.tensor(run_stream("tensor"))
        block.gpsimd(run_stream("gpsimd"))
    return nc


def _colblocks(WT, nb):
    K, N = WT.shape
    assert N == nb * 128
    return np.ascontiguousarray(WT.reshape(K // 128, 128, nb, 128).transpose(2, 1, 0, 3))


def _host_inputs(S, L0_attn_Wq, L0_attn_Wk, L0_attn_Wv, L0_attn_Wo,
                 L0_attn_q_norm_w, L0_attn_k_norm_w, L0_attn_Wrq, L0_attn_Wrk,
                 L0_attn_graph_scale, L0_ffn_W0, L0_ffn_bias0, L0_ffn_W1,
                 L0_ffn_bias1):
    f32 = np.float32
    base = {}
    base["wq"] = _colblocks(np.ascontiguousarray(L0_attn_Wq.T), 16)
    base["wk"] = _colblocks(np.ascontiguousarray(L0_attn_Wk.T), 4)
    base["wv"] = np.ascontiguousarray(L0_attn_Wv.T)
    base["wrq"] = _colblocks(np.ascontiguousarray(L0_attn_Wrq.T), 3)
    base["wrk"] = _colblocks(np.ascontiguousarray(L0_attn_Wrk.T), 3)
    base["wo"] = _colblocks(np.ascontiguousarray(L0_attn_Wo.T), 16)
    base["w0"] = _colblocks(np.ascontiguousarray(L0_ffn_W0.T), 64)
    base["w1"] = _colblocks(np.ascontiguousarray(L0_ffn_W1.T), 16)
    qw = np.asarray(L0_attn_q_norm_w, f32)
    kw = np.asarray(L0_attn_k_norm_w, f32)
    base["qnw"] = np.stack([qw, np.roll(qw, -64)], axis=1).astype(f32)
    base["knw"] = np.stack([kw, np.roll(kw, -64)], axis=1).astype(f32)
    expnt = np.arange(0, 128, 2, dtype=f32) / f32(128.0)
    invf = (f32(1.0) / np.power(f32(10000.0), expnt)).astype(f32)
    t = np.arange(S, dtype=f32)
    freqs = np.outer(t, invf).astype(f32)
    emb = np.concatenate([freqs, freqs], axis=1)
    base["costab"] = np.ascontiguousarray(np.cos(emb).T.astype(f32))
    base["sintab"] = np.ascontiguousarray(np.sin(emb).T.astype(f32))
    rotm = np.zeros((128, 128), np.float32)
    for d in range(64):
        rotm[d + 64, d] = -1.0
        rotm[d, d + 64] = 1.0
    base["rotm"] = rotm.astype(BFD)
    base["ident"] = np.eye(128, dtype=BFD)
    mp = np.arange(128)[:, None]
    mm_ = np.arange(896)[None, :]
    base["maskbuf"] = (mm_ >= mp + 384).astype(BFD)
    gs = np.asarray(L0_attn_graph_scale, f32)
    gs2 = np.zeros((128, 3), f32)
    for c in range(3):
        gs2[:, c] = gs[4 * c + np.arange(128) // 32]
    base["gs2"] = gs2
    base["b0r"] = np.ascontiguousarray(np.asarray(L0_ffn_bias0, f32).reshape(64, 128).T)
    base["b1r"] = np.ascontiguousarray(np.asarray(L0_ffn_bias1, f32).reshape(16, 128).T)
    cst = np.zeros((128, 8), f32)
    cst[:, 0] = -NEGC
    cst[:, 1] = 1e-5
    cst[:, 2] = 1e-6
    cst[:, 3] = 1.0 / 2048.0
    cst[:, 4] = 1.0 / 128.0
    cst[:, 5] = 0.5
    cst[:, 6] = 0.0
    cst[:, 7] = 1.0
    base["cst"] = cst
    base["onesc"] = np.ones((128, 1), BFD)
    base["onesr"] = np.ones((1, 128), BFD)
    base["onesm"] = np.ones((128, 128), BFD)
    return base


def kernel(L0_attn_x, L0_attn_Wq, L0_attn_Wk, L0_attn_Wv, L0_attn_Wo,
           L0_attn_q_norm_w, L0_attn_k_norm_w, L0_attn_Wrq, L0_attn_Wrk,
           L0_attn_graph_scale, L0_ffn_W0, L0_ffn_bias0, L0_ffn_W1, L0_ffn_bias1):
    S = L0_attn_x.shape[1]
    base = _host_inputs(S, L0_attn_Wq, L0_attn_Wk, L0_attn_Wv, L0_attn_Wo,
                        L0_attn_q_norm_w, L0_attn_k_norm_w, L0_attn_Wrq,
                        L0_attn_Wrk, L0_attn_graph_scale, L0_ffn_W0,
                        L0_ffn_bias0, L0_ffn_W1, L0_ffn_bias1)
    nc = build(S)
    B = L0_attn_x.shape[0]
    in_maps = []
    for c in range(B):
        m = dict(base)
        xc = np.ascontiguousarray(np.asarray(L0_attn_x[c], np.float32))
        m["x"] = xc
        m["xT"] = np.ascontiguousarray(xc.T)
        in_maps.append(m)
    res = run_bass_kernel_spmd(nc, in_maps, core_ids=list(range(B)))
    out = np.stack([np.ascontiguousarray(r["outT"].T) for r in res.results], axis=0)
    return out


# version 09
# speedup vs baseline: 1.1118x; 1.1017x over previous
import numpy as np
import ml_dtypes
import concourse.bass as bass
import concourse.mybir as mybir
from concourse.bass_utils import run_bass_kernel_spmd

F32 = mybir.dt.float32
BF16 = mybir.dt.bfloat16
AF = mybir.ActivationFunctionType
ALU = mybir.AluOpType
AX = mybir.AxisListType
BFD = ml_dtypes.bfloat16

E = 2048
H = 16
KVH = 4
D = 128
HG = 12
SYM = 4
NF = 8192
NEC = 16
NEGC = 14.0
INV_SQRT_D = float(1.0 / np.sqrt(np.float32(128.0)))

ENGS = ("sync", "vector", "scalar", "tensor", "gpsimd")


class Sem:
    def __init__(self, name):
        self.name = name
        self.n = 0
        self.h = None


def build(S, n_dbg=0, STAG=2):
    NT = S // 128
    NSC = S // 512
    nc = bass.Bass("TRN2", target_bir_lowering=False, debug=False)

    d_x = nc.dram_tensor("x", [S, E], F32, kind="ExternalInput").ap()
    d_xT = nc.dram_tensor("xT", [E, S], F32, kind="ExternalInput").ap()
    d_wq = nc.dram_tensor("wq", [H, 128, NEC, 128], F32, kind="ExternalInput").ap()
    d_wk = nc.dram_tensor("wk", [KVH, 128, NEC, 128], F32, kind="ExternalInput").ap()
    d_wv = nc.dram_tensor("wv", [E, 512], F32, kind="ExternalInput").ap()
    d_wrq = nc.dram_tensor("wrq", [3, 128, NEC, 128], F32, kind="ExternalInput").ap()
    d_wrk = nc.dram_tensor("wrk", [3, 128, NEC, 128], F32, kind="ExternalInput").ap()
    d_wo = nc.dram_tensor("wo", [16, 128, NEC, 128], F32, kind="ExternalInput").ap()
    d_w0 = nc.dram_tensor("w0", [64, 128, NEC, 128], F32, kind="ExternalInput").ap()
    d_w1 = nc.dram_tensor("w1", [16, 128, 64, 128], F32, kind="ExternalInput").ap()
    d_qnw = nc.dram_tensor("qnw", [128, 2], F32, kind="ExternalInput").ap()
    d_knw = nc.dram_tensor("knw", [128, 2], F32, kind="ExternalInput").ap()
    d_cos = nc.dram_tensor("costab", [128, S], F32, kind="ExternalInput").ap()
    d_sin = nc.dram_tensor("sintab", [128, S], F32, kind="ExternalInput").ap()
    d_rotm = nc.dram_tensor("rotm", [128, 128], BF16, kind="ExternalInput").ap()
    d_ident = nc.dram_tensor("ident", [128, 128], BF16, kind="ExternalInput").ap()
    d_mask = nc.dram_tensor("maskbuf", [128, 896], BF16, kind="ExternalInput").ap()
    d_gs = nc.dram_tensor("gs2", [128, 3], F32, kind="ExternalInput").ap()
    d_b0 = nc.dram_tensor("b0r", [128, 64], F32, kind="ExternalInput").ap()
    d_b1 = nc.dram_tensor("b1r", [128, 16], F32, kind="ExternalInput").ap()
    d_cst = nc.dram_tensor("cst", [128, 10], F32, kind="ExternalInput").ap()
    d_onesc = nc.dram_tensor("onesc", [128, 1], BF16, kind="ExternalInput").ap()
    d_onesr = nc.dram_tensor("onesr", [1, 128], BF16, kind="ExternalInput").ap()
    d_onesm = nc.dram_tensor("onesm", [128, 128], BF16, kind="ExternalInput").ap()
    d_y1sp = nc.dram_tensor("y1sp", [E, S], BF16, kind="Internal").ap()
    d_wbf = nc.dram_tensor("wbf", [160, 128, NEC, 128], BF16, kind="Internal").ap()
    d_out = nc.dram_tensor("outT", [E, S], F32, kind="ExternalOutput").ap()

    C_NEGC, C_EPSLN, C_EPSRMS, C_INV_E, C_INV_D, C_HALF, C_ZERO, C_ONE, \
        C_NEGH, C_NEG1 = range(10)

    KT_O = 0
    V_O = KT_O + 4 * S
    RSQ_O = V_O + NT * 512
    RSK_O = RSQ_O + 3 * S
    QT_O = RSK_O + 3 * S
    AT_O = QT_O + 8 * 512
    PB_O = AT_O + 16 * 512
    AM_O = PB_O + 4 * 512
    TB_O = AM_O + 9 * 512
    HACT_O = 0
    W1S_O = 64 * 512
    Y1T_O = W1S_O + 4 * 2048
    ARENA = max(TB_O + 2 * 512, Y1T_O + 2 * 512)
    SQ_O = 0
    XN_O = 2048
    TK_O = QT_O

    AM_QSQ = AM_O
    AM_QRAW = AM_O + 512
    AM_T1 = AM_O + 2 * 512
    AM_RSBC = AM_O + 3 * 512
    AM_RQS = AM_O + 4 * 512
    AM_RKS = AM_O + 5 * 512
    AM_Y1 = AM_O + 6 * 512
    AM_RS2 = AM_O + 8 * 512

    prog = {e: [] for e in ENGS}
    sLD = Sem("sLD")
    sST = Sem("sST")
    sL2 = Sem("sL2")
    sPE = Sem("sPE")
    sAC = Sem("sAC")
    sDV = Sem("sDV")
    sGP = Sem("sGP")
    sems = [sLD, sST, sL2, sPE, sAC, sDV, sGP]

    def op(eng, fn, *waits):
        prog[eng].append((tuple(waits), fn))

    def dma(q, dst, src, *waits):
        sem = {"sync": sLD, "gpsimd": sST, "scalar": sL2}[q]
        def fn(e, dst=dst, src=src, sem=sem):
            e.dma_start(dst, src).then_inc(sem.h, 16)
        op(q, fn, (sem, sem.n), *waits)
        sem.n += 16
        return sem.n

    def pe(fn, *waits, inc=False):
        if inc:
            def fn2(e, fn=fn):
                fn(e).then_inc(sPE.h, 1)
            op("tensor", fn2, *waits)
            sPE.n += 1
            return sPE.n
        op("tensor", fn, *waits)
        return sPE.n

    def flush(eng, *waits):
        sem = {"vector": sDV, "scalar": sAC, "gpsimd": sGP}[eng]
        def fn(e, sem=sem):
            e.drain().then_inc(sem.h, 1)
        op(eng, fn, *waits)
        sem.n += 1
        return sem.n

    import contextlib
    cx = contextlib.ExitStack()
    EC = cx.enter_context
    xnT = EC(nc.sbuf_tensor("xnT", [128, NEC * S], BF16))
    ar = EC(nc.sbuf_tensor("ar", [128, ARENA], BF16))
    wcbf = EC(nc.sbuf_tensor("wcbf", [128, 2, 2048], F32))
    wcbb = EC(nc.sbuf_tensor("wcbb", [128, 2 * 2048], BF16))
    fst = EC(nc.sbuf_tensor("fst", [128, 3 * 512], F32))
    xtb = EC(nc.sbuf_tensor("xtb", [128, 2 * 512], F32))
    rowf = EC(nc.sbuf_tensor("rowf", [128, 1024], F32))
    rbf = EC(nc.sbuf_tensor("rbf", [1, 512], BF16))
    cstS = EC(nc.sbuf_tensor("cstS", [128, 10], F32))
    qnwS = EC(nc.sbuf_tensor("qnwS", [128, 2], F32))
    knwS = EC(nc.sbuf_tensor("knwS", [128, 2], F32))
    gsS = EC(nc.sbuf_tensor("gsS", [128, 4], F32))
    b0S = EC(nc.sbuf_tensor("b0S", [128, 64], F32))
    b1S = EC(nc.sbuf_tensor("b1S", [128, 16], F32))
    lnS = EC(nc.sbuf_tensor("lnS", [128, 8], F32))
    rotmS = EC(nc.sbuf_tensor("rotmS", [128, 128], BF16))
    identS = EC(nc.sbuf_tensor("identS", [128, 128], BF16))
    maskS = EC(nc.sbuf_tensor("maskS", [128, 896], BF16))
    onescS = EC(nc.sbuf_tensor("onescS", [128, 1], BF16))
    onesrS = EC(nc.sbuf_tensor("onesrS", [1, 128], BF16))
    onesmS = EC(nc.sbuf_tensor("onesmS", [128, 128], BF16))
    ps = [EC(nc.psum_tensor("ps%d" % i, [128, 512], F32)) for i in range(8)]

    wstate = {"wfi": 0}
    wslot_free = [(sDV, 0), (sDV, 0)]
    WSP = {}

    def feed32(src_ap, width=2048, q="sync", *extra):
        sl = wstate["wfi"] % 2
        wstate["wfi"] += 1
        v = dma(q, wcbf[:, sl, 0:width], src_ap, wslot_free[sl], *extra)
        return sl, v

    v0 = dma("sync", cstS[:, :], d_cst[:, :])
    dma("sync", qnwS[:, :], d_qnw[:, :])
    dma("sync", knwS[:, :], d_knw[:, :])
    dma("sync", gsS[:, 0:3], d_gs[:, :])
    dma("sync", b0S[:, :], d_b0[:, :])
    dma("sync", b1S[:, :], d_b1[:, :])
    dma("sync", rotmS[:, :], d_rotm[:, :])
    dma("sync", identS[:, :], d_ident[:, :])
    dma("sync", maskS[:, :], d_mask[:, :])
    dma("sync", onescS[:, :], d_onesc[:, :])
    dma("sync", onesrS[:, :], d_onesr[:, :])
    v_const = dma("sync", onesmS[:, :], d_onesm[:, :])
    def f_gshalf(e):
        e.tensor_scalar(gsS[:, 3:4], gsS[:, 0:1], 0.5, None, ALU.mult)
    op("vector", f_gshalf, (sLD, v_const))
    v_gs = flush("vector")

    xn_done_pe = 0
    last_norm_dv = 0
    for st in range(NT):
        xsl, v_x = feed32(d_x[st * 128:(st + 1) * 128, :])
        def f_red(e, xsl=xsl):
            e.reduce_sum(lnS[:, 0:1], wcbf[:, xsl, :], axis=AX.X)
        op("vector", f_red, (sLD, v_x))
        v_red = flush("vector")
        def f_sq(e, xsl=xsl):
            e.activation(ar[:, SQ_O:SQ_O + 2048], wcbf[:, xsl, :], AF.Square,
                         accum_out=lnS[:, 1:2])
        op("scalar", f_sq, (sLD, v_x))
        v_sq = flush("scalar")
        def f_ln1(e):
            e.tensor_scalar(lnS[:, 2:3], lnS[:, 0:1], 1.0 / 2048.0, None, ALU.mult)
            e.tensor_scalar(lnS[:, 3:4], lnS[:, 1:2], 1.0 / 2048.0,
                            cstS[:, C_EPSLN:C_EPSLN + 1], ALU.mult, ALU.add)
        op("vector", f_ln1, (sDV, v_red), (sAC, v_sq))
        flush("vector")
        def f_ln2(e):
            e.scalar_tensor_tensor(lnS[:, 4:5], lnS[:, 2:3], -1.0, lnS[:, 2:3],
                                   ALU.mult, ALU.mult)
        op("vector", f_ln2)
        flush("vector")
        def f_ln3(e):
            e.tensor_tensor(lnS[:, 5:6], lnS[:, 3:4], lnS[:, 4:5], ALU.add)
        op("vector", f_ln3)
        flush("vector")
        def f_ln4(e):
            e.reciprocal(lnS[:, 6:7], lnS[:, 5:6])
        op("vector", f_ln4)
        v_rec = flush("vector")
        def f_ln5(e):
            e.activation(lnS[:, 7:8], lnS[:, 6:7], AF.Sqrt)
        op("scalar", f_ln5, (sDV, v_rec))
        v_rs = flush("scalar")
        def f_ln6(e):
            e.scalar_tensor_tensor(lnS[:, 4:5], lnS[:, 2:3], -1.0, lnS[:, 7:8],
                                   ALU.mult, ALU.mult)
        op("vector", f_ln6, (sAC, v_rs))
        flush("vector")
        xslot = XN_O + (st % 2) * 2048
        def f_norm(e, xslot=xslot, xsl=xsl):
            e.tensor_scalar(ar[:, xslot:xslot + 2048], wcbf[:, xsl, :],
                            lnS[:, 7:8], lnS[:, 4:5], ALU.mult, ALU.add)
        op("vector", f_norm, (sPE, xn_done_pe), (sAC, v_sq))
        last_norm_dv = flush("vector")
        wslot_free[xsl] = (sDV, last_norm_dv)
        evac_dv = {}
        for g4 in range(4):
            bank = ps[6 + (g4 % 2)]
            w_ev = evac_dv.get(g4 - 2, 0)
            for k in range(4):
                ecx = g4 * 4 + k
                def f_tr(e, bank=bank, xslot=xslot, ecx=ecx, k=k):
                    return e.matmul(bank[:, k * 128:(k + 1) * 128],
                                    ar[:, xslot + ecx * 128:xslot + (ecx + 1) * 128],
                                    identS[:, :], start=True, stop=True)
                v_pe = pe(f_tr, (sDV, last_norm_dv), (sDV, w_ev),
                          inc=(k == 3))
            def f_ev(e, bank=bank, g4=g4, st=st):
                for k in range(4):
                    ecx = g4 * 4 + k
                    e.tensor_copy(xnT[:, ecx * S + st * 128: ecx * S + (st + 1) * 128],
                                  bank[:, k * 128:(k + 1) * 128])
            op("vector", f_ev, (sPE, v_pe))
            evac_dv[g4] = flush("vector")
        xn_done_pe = sPE.n

    v_xnT_dv = sDV.n

    csl, v_c = feed32(d_cos[:, :], S)
    def f_foldkc(e, csl=csl):
        e.tensor_scalar(ar[:, TK_O:TK_O + S], wcbf[:, csl, 0:S],
                        knwS[:, 0:1], None, ALU.mult)
    op("vector", f_foldkc, (sLD, v_c))
    v_fkc = flush("vector")
    wslot_free[csl] = (sDV, v_fkc)
    ssl, v_s = feed32(d_sin[:, :], S)
    def f_foldks(e, ssl=ssl):
        e.tensor_scalar(ar[:, TK_O + S:TK_O + 2 * S], wcbf[:, ssl, 0:S],
                        knwS[:, 1:2], None, ALU.mult)
    op("vector", f_foldks, (sLD, v_s))
    v_fks = flush("vector")
    wslot_free[ssl] = (sDV, v_fks)

    ps2_free = (sDV, 0)
    wcbb_free = [((sPE, 0),), ((sPE, 0),)]
    psum_free = [(sDV, 0), (sDV, 0)]
    cb_list = [("k", g) for g in range(KVH)] + \
              [("rq", rc) for rc in range(3)] + [("rk", rc) for rc in range(3)]
    it = 0
    for kind, idx in cb_list:
        src = {"k": d_wk, "rq": d_wrq, "rk": d_wrk}[kind]
        fsl, v_w = feed32(src[idx, :, :, :])
        slot = it % 2
        def f_cast(e, slot=slot, fsl=fsl):
            e.tensor_copy(wcbb[:, slot * 2048:(slot + 1) * 2048], wcbf[:, fsl, :])
        op("vector", f_cast, (sLD, v_w), *wcbb_free[slot])
        v_cast = flush("vector")
        wslot_free[fsl] = (sDV, v_cast)
        for tc in range(NSC):
            par = (it * NSC + tc) % 2
            bank = ps[par]
            for ec in range(NEC):
                def f_mm(e, bank=bank, slot=slot, ec=ec, tc=tc):
                    return e.matmul(bank[:, :],
                                    wcbb[:, slot * 2048 + ec * 128:
                                         slot * 2048 + (ec + 1) * 128],
                                    xnT[:, ec * S + tc * 512: ec * S + tc * 512 + 512],
                                    start=(ec == 0), stop=(ec == NEC - 1))
                v_mm = pe(f_mm, (sDV, v_cast), psum_free[par],
                          (sDV, v_xnT_dv) if ec == 0 and it == 0 else (sDV, 0),
                          inc=(ec == NEC - 1))
            if kind == "k":
                g, tcp = idx, tc
                def f_ksq(e, bank=bank):
                    e.activation(ar[:, AM_QSQ:AM_QSQ + 512], bank[:, :], AF.Square)
                op("scalar", f_ksq, (sPE, v_mm))
                v_ksq = flush("scalar")
                def f_dmm(e):
                    return e.matmul(ps[2][:, :], onesmS[:, :],
                                    ar[:, AM_QSQ:AM_QSQ + 512], start=True, stop=True)
                v_dmm = pe(f_dmm, (sAC, v_ksq), ps2_free, inc=True)
                def f_r1(e):
                    e.activation(rowf[:, 0:512], ps[2][:, :], AF.Ln,
                                 bias=cstS[:, C_EPSRMS:C_EPSRMS + 1],
                                 scale=cstS[:, C_INV_D:C_INV_D + 1])
                op("scalar", f_r1, (sPE, v_dmm))
                ps2_free = (sAC, flush("scalar"))
                def f_r3(e):
                    e.activation(rowf[:, 512:1024], rowf[:, 0:512], AF.Exp,
                                 scale=cstS[:, C_NEGH:C_NEGH + 1])
                op("scalar", f_r3)
                v_r3 = flush("scalar")
                def f_kraw(e, bank=bank):
                    e.tensor_tensor(ar[:, AM_QRAW:AM_QRAW + 512], bank[:, :],
                                    rowf[:, 512:1024], ALU.mult)
                op("vector", f_kraw, (sAC, v_r3))
                v_kraw = flush("vector")
                psum_free[par] = (sDV, v_kraw)
                def f_rot(e):
                    return e.matmul(ps[3][:, :], rotmS[:, :],
                                    ar[:, AM_QRAW:AM_QRAW + 512], start=True, stop=True)
                v_rot = pe(f_rot, (sDV, v_kraw), inc=True)
                def f_t1(e, tcp=tcp):
                    e.tensor_tensor(ar[:, AM_T1:AM_T1 + 512],
                                    ar[:, AM_QRAW:AM_QRAW + 512],
                                    ar[:, TK_O + tcp * 512:TK_O + tcp * 512 + 512],
                                    ALU.mult)
                op("vector", f_t1)
                flush("vector")
                def f_t2(e, tcp=tcp):
                    e.tensor_tensor(ar[:, AM_QRAW:AM_QRAW + 512], ps[3][:, :],
                                    ar[:, TK_O + S + tcp * 512:TK_O + S + tcp * 512 + 512],
                                    ALU.mult)
                op("vector", f_t2, (sPE, v_rot))
                flush("vector")
                def f_kT(e, g=g, tcp=tcp):
                    e.tensor_tensor(ar[:, KT_O + g * S + tcp * 512:
                                       KT_O + g * S + tcp * 512 + 512],
                                    ar[:, AM_T1:AM_T1 + 512],
                                    ar[:, AM_QRAW:AM_QRAW + 512], ALU.add)
                op("vector", f_kT)
                flush("vector")
            else:
                dst_o = (RSQ_O if kind == "rq" else RSK_O) + idx * S + tc * 512
                if kind == "rk":
                    gcol = 3 if idx == 0 else idx
                    def f_ev(e, bank=bank, dst_o=dst_o, gcol=gcol):
                        e.tensor_scalar(ar[:, dst_o:dst_o + 512], bank[:, :],
                                        gsS[:, gcol:gcol + 1], None, ALU.mult)
                    op("vector", f_ev, (sPE, v_mm), (sDV, v_gs))
                else:
                    def f_ev(e, bank=bank, dst_o=dst_o):
                        e.tensor_copy(ar[:, dst_o:dst_o + 512], bank[:, :])
                    op("vector", f_ev, (sPE, v_mm))
                psum_free[par] = (sDV, flush("vector"))
        wcbb_free[slot] = ((sPE, sPE.n),)
        it += 1

    vps_free = [(sAC, 0), (sAC, 0), (sDV, v_xnT_dv), (sDV, v_xnT_dv)]
    for ttg in range(NT // 4):
        for ec in range(NEC):
            fsl, v_w = feed32(d_wv[ec * 128:(ec + 1) * 128, :], 512)
            slot = ec % 2
            def f_cast(e, slot=slot, fsl=fsl):
                e.tensor_copy(wcbb[:, slot * 2048:slot * 2048 + 512],
                              wcbf[:, fsl, 0:512])
            op("vector", f_cast, (sLD, v_w), *wcbb_free[slot])
            v_cast = flush("vector")
            wslot_free[fsl] = (sDV, v_cast)
            for j in range(4):
                tt = ttg * 4 + j
                def f_mm(e, j=j, slot=slot, ec=ec, tt=tt):
                    return e.matmul(ps[4 + j][:, :],
                                    xnT[:, ec * S + tt * 128: ec * S + (tt + 1) * 128],
                                    wcbb[:, slot * 2048:slot * 2048 + 512],
                                    start=(ec == 0), stop=(ec == NEC - 1))
                v_mm = pe(f_mm, (sDV, v_cast), vps_free[j],
                          inc=(j == 3 or ec == NEC - 1))
            wcbb_free[slot] = ((sPE, sPE.n),)
        for j in range(4):
            tt = ttg * 4 + j
            def f_ev(e, j=j, tt=tt):
                e.activation(ar[:, V_O + tt * 512:V_O + (tt + 1) * 512],
                             ps[4 + j][:, :], AF.Copy)
            op("scalar", f_ev, (sPE, v_mm))
            vps_free[j] = (sAC, flush("scalar"))

    a0_end_pe = sPE.n
    a0_end_dv = sDV.n
    a0_end_ac = sAC.n

    y1_spills = []
    wcbb_free_m = wcbb_free
    tab_dv = 0
    pslot_free = [(sPE, 0)] * 4
    qps_free = [(sAC, a0_end_ac), (sAC, a0_end_ac)]
    sc_ps_free = [(sDV, a0_end_dv), (sDV, a0_end_dv)]
    av_free = [(sAC, a0_end_ac), (sAC, a0_end_ac)]
    qT_free = [(sPE, 0)] * 8
    y1st_free = [(sST, 0), (sST, 0)]
    ps3_free = (sDV, 0)
    pend_end = None
    rope_q = []
    rdn_free = [(sDV, 0)]
    rs_free = [(sDV, 0), (sDV, 0), (sDV, 0)]
    sqr_free = [(sDV, 0), (sDV, 0)]
    spill_q = []
    for j in range(64):
        spill_q.append((j, lambda j=j: d_w0[j, :, :, :]))
    for eoc in range(16):
        for qq in range(4):
            spill_q.append((64 + eoc * 4 + qq,
                            lambda eoc=eoc, qq=qq: d_w1[eoc, :, qq * 16:(qq + 1) * 16, :]))
    spill_i = [0]
    sb_slot = [0]

    def emit_spills(k):
        for _ in range(k):
            if spill_i[0] >= len(spill_q):
                return
            blk, srcfn = spill_q[spill_i[0]]
            spill_i[0] += 1
            fsl, v_w = feed32(srcfn(), 2048, "scalar")
            bsl = sb_slot[0] % 2
            sb_slot[0] += 1
            def f_cast(e, bsl=bsl, fsl=fsl):
                e.tensor_copy(wcbb[:, bsl * 2048:(bsl + 1) * 2048], wcbf[:, fsl, :])
            op("vector", f_cast, (sL2, v_w), *wcbb_free_m[bsl])
            v_cast = flush("vector")
            wslot_free[fsl] = (sDV, v_cast)
            WSP[blk] = dma("gpsimd", d_wbf[blk, :, :, :],
                           wcbb[:, bsl * 2048:(bsl + 1) * 2048], (sDV, v_cast))
            wcbb_free_m[bsl] = ((sST, WSP[blk]),)
    aT_read_pe = 0
    for sc in range(NSC):
        nt = 4 * (sc + 1)
        v_c = dma("sync", fst[:, 0:512], d_cos[:, sc * 512:(sc + 1) * 512],
                  (sDV, sDV.n))
        def f_fqc(e):
            e.tensor_scalar(ar[:, TB_O:TB_O + 512], fst[:, 0:512],
                            qnwS[:, 0:1], INV_SQRT_D, ALU.mult, ALU.mult)
        op("vector", f_fqc, (sLD, v_c))
        v_fq = flush("vector")
        v_s2 = dma("sync", fst[:, 0:512], d_sin[:, sc * 512:(sc + 1) * 512],
                   (sDV, v_fq))
        def f_fqs(e):
            e.tensor_scalar(ar[:, TB_O + 512:TB_O + 1024], fst[:, 0:512],
                            qnwS[:, 1:2], INV_SQRT_D, ALU.mult, ALU.mult)
        op("vector", f_fqs, (sLD, v_s2))
        tab_dv = flush("vector")

        for half in range(2):
            for hh in range(8):
                h = half * 8 + hh
                slot = h % 2
                if sc == 0:
                    fsl, v_w = feed32(d_wq[h, :, :, :])
                    def f_cast(e, slot=slot, fsl=fsl):
                        e.tensor_copy(wcbb[:, slot * 2048:(slot + 1) * 2048],
                                      wcbf[:, fsl, :])
                    op("vector", f_cast, (sLD, v_w), *wcbb_free_m[slot])
                    v_cast = flush("vector")
                    wslot_free[fsl] = (sDV, v_cast)
                    WSP[128 + h] = dma("gpsimd", d_wbf[128 + h, :, :, :],
                                       wcbb[:, slot * 2048:(slot + 1) * 2048],
                                       (sDV, v_cast))
                    mm_w = (sDV, v_cast)
                else:
                    v_w = dma("sync", wcbb[:, slot * 2048:(slot + 1) * 2048],
                              d_wbf[128 + h, :, :, :], *wcbb_free_m[slot],
                              (sST, WSP[128 + h]))
                    mm_w = (sLD, v_w)
                par = h % 2
                bank = ps[6 + par]
                for ec in range(NEC):
                    def f_mm(e, bank=bank, slot=slot, ec=ec, sc=sc):
                        return e.matmul(bank[:, :],
                                        wcbb[:, slot * 2048 + ec * 128:
                                             slot * 2048 + (ec + 1) * 128],
                                        xnT[:, ec * S + sc * 512: ec * S + sc * 512 + 512],
                                        start=(ec == 0), stop=(ec == NEC - 1))
                    v_mm = pe(f_mm, mm_w, qps_free[par],
                              inc=(ec == NEC - 1))
                wcbb_free_m[slot] = (((sPE, sPE.n),) if sc > 0 else
                                     ((sPE, sPE.n), (sST, WSP[128 + h])))
                def f_qsq(e, bank=bank):
                    e.activation(ar[:, AM_QSQ:AM_QSQ + 512], bank[:, :], AF.Square)
                op("scalar", f_qsq, (sPE, v_mm), (sDV, tab_dv))
                v_qsq = flush("scalar")
                def f_dmm(e):
                    return e.matmul(ps[2][:, :], onesmS[:, :],
                                    ar[:, AM_QSQ:AM_QSQ + 512], start=True, stop=True)
                v_dmm = pe(f_dmm, (sAC, v_qsq), ps2_free, inc=True)
                sqslot = rowf[:, 0:512] if hh % 2 == 0 else xtb[:, 0:512]
                def f_r1(e, sqslot=sqslot):
                    e.activation(sqslot, ps[2][:, :], AF.Ln,
                                 bias=cstS[:, C_EPSRMS:C_EPSRMS + 1],
                                 scale=cstS[:, C_INV_D:C_INV_D + 1])
                op("scalar", f_r1, (sPE, v_dmm), sqr_free[hh % 2])
                ps2_free = (sAC, flush("scalar"))
                rs_o = (hh % 3) * 512
                def f_r3(e, rs_o=rs_o, sqslot=sqslot):
                    e.activation(fst[:, rs_o:rs_o + 512], sqslot, AF.Exp,
                                 scale=cstS[:, C_NEGH:C_NEGH + 1])
                op("scalar", f_r3, rs_free[hh % 3])
                v_r3 = flush("scalar")
                sqr_free[hh % 2] = (sAC, v_r3)

                def mk_rope(hh=hh, h=h, bank=bank, par=par, rs_o=rs_o, v_r3=v_r3):
                    def doit():
                        nonlocal qps_free
                        def f_qraw(e, bank=bank, rs_o=rs_o):
                            e.tensor_tensor(ar[:, AM_QRAW:AM_QRAW + 512], bank[:, :],
                                            fst[:, rs_o:rs_o + 512], ALU.mult)
                        op("vector", f_qraw, (sAC, v_r3))
                        rs_free[rs_o // 512] = (sDV, sDV.n + 1)
                        v_qraw = flush("vector")
                        qps_free[par] = (sDV, v_qraw)
                        def f_rot(e):
                            return e.matmul(ps[3][:, :], rotmS[:, :],
                                            ar[:, AM_QRAW:AM_QRAW + 512],
                                            start=True, stop=True)
                        v_rot = pe(f_rot, (sDV, v_qraw), inc=True)
                        def f_t1(e):
                            e.tensor_tensor(ar[:, AM_T1:AM_T1 + 512],
                                            ar[:, AM_QRAW:AM_QRAW + 512],
                                            ar[:, TB_O:TB_O + 512], ALU.mult)
                        op("vector", f_t1)
                        flush("vector")
                        def f_t2(e):
                            e.tensor_tensor(ar[:, AM_QRAW:AM_QRAW + 512], ps[3][:, :],
                                            ar[:, TB_O + 512:TB_O + 1024], ALU.mult)
                        op("vector", f_t2, (sPE, v_rot))
                        flush("vector")
                        def f_qT(e, hh=hh):
                            e.tensor_tensor(ar[:, QT_O + hh * 512:QT_O + hh * 512 + 512],
                                            ar[:, AM_T1:AM_T1 + 512],
                                            ar[:, AM_QRAW:AM_QRAW + 512], ALU.add)
                        op("vector", f_qT, qT_free[hh])
                        qT_free[hh] = (sDV, flush("vector"))
                    return doit
                rope_q.append(mk_rope())
                if len(rope_q) > 1:
                    rope_q.pop(0)()
            while rope_q:
                rope_q.pop(0)()

            for hh in range(8):
                h = half * 8 + hh
                g = h // 4
                jo = 32 * (h % 4)
                av_q = []
                dn = {}
                for tt in range(nt):
                    par = tt % 2
                    bank = ps[par]
                    def f_qk(e, bank=bank, g=g, tt=tt, hh=hh, h=h):
                        return e.matmul(bank[:, :],
                                        ar[:, KT_O + g * S + tt * 128:
                                           KT_O + g * S + (tt + 1) * 128],
                                        ar[:, QT_O + hh * 512:QT_O + hh * 512 + 512],
                                        start=True, stop=(h >= HG))
                    v_sc = pe(f_qk, sc_ps_free[par], qT_free[hh],
                              inc=(h >= HG))
                    if h < HG:
                        issym = h < SYM
                        def f_b1m(e, bank=bank, h=h, jo=jo, tt=tt, issym=issym,
                                  sc=sc):
                            return e.matmul(bank[:, :],
                                            ar[jo:jo + 32, RSK_O + (h // 4) * S + tt * 128:
                                               RSK_O + (h // 4) * S + (tt + 1) * 128],
                                            ar[jo:jo + 32, RSQ_O + (h // 4) * S + sc * 512:
                                               RSQ_O + (h // 4) * S + sc * 512 + 512],
                                            start=False, stop=(not issym),
                                            tile_position=(jo, 0))
                        v_sc = pe(f_b1m, inc=(not issym))
                        if issym:
                            def f_b2m(e, bank=bank, h=h, jo=jo, tt=tt, sc=sc):
                                return e.matmul(bank[:, :],
                                                ar[jo:jo + 32, RSQ_O + (h // 4) * S + tt * 128:
                                                   RSQ_O + (h // 4) * S + (tt + 1) * 128],
                                                ar[jo:jo + 32, RSK_O + (h // 4) * S + sc * 512:
                                                   RSK_O + (h // 4) * S + sc * 512 + 512],
                                                start=False, stop=True,
                                                tile_position=(jo, 0))
                            v_sc = pe(f_b2m, inc=True)
                    slot = tt % 4
                    def f_exp(e, bank=bank, slot=slot):
                        e.activation(ar[:, PB_O + slot * 512:PB_O + (slot + 1) * 512],
                                     bank[:, :], AF.Exp, bias=cstS[:, C_NEGC:C_NEGC + 1])
                    op("scalar", f_exp, (sPE, v_sc), pslot_free[slot])
                    v_exp = flush("scalar")
                    sc_ps_free[par] = (sAC, v_exp)
                    p_ready = (sAC, v_exp)
                    if tt >= 4 * sc:
                        off = 128 * tt - 512 * sc
                        def f_msk(e, slot=slot, off=off):
                            e.tensor_tensor(ar[:, PB_O + slot * 512:PB_O + (slot + 1) * 512],
                                            ar[:, PB_O + slot * 512:PB_O + (slot + 1) * 512],
                                            maskS[:, 384 - off:384 - off + 512], ALU.mult)
                        op("gpsimd", f_msk, (sAC, v_exp))
                        p_ready = (sGP, flush("gpsimd"))
                    avb = ps[4 + (h % 2)]
                    def mk_avden(tt=tt, slot=slot, p_ready=p_ready, nt=nt,
                                 avb=avb, g=g, h=h):
                        def doit():
                            def f_av(e, avb=avb, tt=tt, g=g, slot=slot, nt=nt):
                                return e.matmul(avb[:, :],
                                                ar[:, V_O + tt * 512 + g * 128:
                                                   V_O + tt * 512 + (g + 1) * 128],
                                                ar[:, PB_O + slot * 512:PB_O + (slot + 1) * 512],
                                                start=(tt == 0), stop=(tt == nt - 1))
                            pe(f_av, p_ready, av_free[h % 2] if tt == 0 else (sPE, 0))
                            def f_den(e, tt=tt, slot=slot, nt=nt):
                                return e.matmul(ps[2][:, :], onesmS[:, :],
                                                ar[:, PB_O + slot * 512:PB_O + (slot + 1) * 512],
                                                start=(tt == 0), stop=(tt == nt - 1))
                            v = pe(f_den, ps2_free if tt == 0 else (sPE, 0), inc=True)
                            pslot_free[slot] = (sPE, v)
                            dn["v"] = v
                        return doit
                    av_q.append(mk_avden())
                    if tt >= 2:
                        av_q[tt - 2]()
                if nt >= 2:
                    av_q[nt - 2]()
                av_q[nt - 1]()
                v_avden = dn["v"]
                def f_rd1(e):
                    e.activation(xtb[:, 512:1024], ps[2][:, :], AF.Ln)
                op("scalar", f_rd1, (sPE, v_avden))
                ps2_free = (sAC, flush("scalar"))
                def f_rd2(e):
                    e.activation(rowf[:, 512:1024], xtb[:, 512:1024], AF.Exp,
                                 scale=cstS[:, C_NEG1:C_NEG1 + 1])
                op("scalar", f_rd2, rdn_free[0])
                v_rd2 = flush("scalar")
                def f_aT(e, h=h, avb=avb):
                    e.tensor_tensor(ar[:, AT_O + h * 512:AT_O + h * 512 + 512],
                                    avb[:, :], rowf[:, 512:1024], ALU.mult)
                op("vector", f_aT, (sAC, v_rd2))
                v_aTf = flush("vector")
                av_free[h % 2] = (sDV, v_aTf)
                rdn_free[0] = (sDV, v_aTf)
                emit_spills([1, 2, 3, 3][sc] if S == 2048 else 8)
        v_aT_all = sDV.n

        for eoc in range(16):
            slot = eoc % 2
            if sc == 0:
                fsl, v_w = feed32(d_wo[eoc, :, :, :])
                def f_cast(e, slot=slot, fsl=fsl):
                    e.tensor_copy(wcbb[:, slot * 2048:(slot + 1) * 2048],
                                  wcbf[:, fsl, :])
                op("vector", f_cast, (sLD, v_w), *wcbb_free_m[slot])
                v_cast = flush("vector")
                wslot_free[fsl] = (sDV, v_cast)
                WSP[144 + eoc] = dma("gpsimd", d_wbf[144 + eoc, :, :, :],
                                     wcbb[:, slot * 2048:(slot + 1) * 2048],
                                     (sDV, v_cast))
                mm_w = (sDV, v_cast)
            else:
                v_w = dma("sync", wcbb[:, slot * 2048:(slot + 1) * 2048],
                          d_wbf[144 + eoc, :, :, :], *wcbb_free_m[slot],
                          (sST, WSP[144 + eoc]))
                mm_w = (sLD, v_w)
            par = eoc % 2
            bank = ps[6 + par]
            for ec in range(NEC):
                def f_mm(e, bank=bank, slot=slot, ec=ec):
                    return e.matmul(bank[:, :],
                                    wcbb[:, slot * 2048 + ec * 128:
                                         slot * 2048 + (ec + 1) * 128],
                                    ar[:, AT_O + ec * 512:AT_O + (ec + 1) * 512],
                                    start=(ec == 0), stop=(ec == NEC - 1))
                v_mm = pe(f_mm, mm_w, qps_free[par], (sDV, v_aT_all),
                          inc=(ec == NEC - 1))
            wcbb_free_m[slot] = (((sPE, sPE.n),) if sc > 0 else
                                 ((sPE, sPE.n), (sST, WSP[144 + eoc])))
            y1slot = AM_Y1 + (eoc % 2) * 512
            def f_ev(e, bank=bank, y1slot=y1slot):
                e.activation(ar[:, y1slot:y1slot + 512], bank[:, :], AF.Copy)
            op("scalar", f_ev, (sPE, v_mm), y1st_free[eoc % 2])
            v_ev = flush("scalar")
            qps_free[par] = (sAC, v_ev)
            v_sp = dma("gpsimd", d_y1sp[eoc * 128:(eoc + 1) * 128, sc * 512:sc * 512 + 512],
                       ar[:, y1slot:y1slot + 512], (sAC, v_ev))
            y1st_free[eoc % 2] = (sST, v_sp)
        aT_read_pe = sPE.n
    a_end_pe = sPE.n
    a_end_dv = sDV.n
    a_end_ac = sAC.n

    wcbb_free_b = wcbb_free_m
    y1sp_done = sST.n
    hact_free_pe = a_end_pe
    w0ps_free = [(sAC, a_end_ac), (sAC, a_end_ac)]
    w1ps_free = [(sDV, a_end_dv), (sDV, a_end_dv)]
    xtb_free = [(sDV, a_end_dv), (sDV, a_end_dv)]
    y1t_free = [(sDV, 0), (sDV, 0)]
    outst_free = [(sST, 0), (sST, 0)]
    for sc in range(NSC):
        for j in range(64):
            slot = j % 2
            v_w = dma("sync", wcbb[:, slot * 2048:(slot + 1) * 2048],
                      d_wbf[j, :, :, :], *wcbb_free_b[slot], (sST, WSP[j]),
                      (sPE, a_end_pe))
            par = j % 2
            bank = ps[par]
            for ec in range(NEC):
                def f_mm(e, bank=bank, slot=slot, ec=ec, sc=sc):
                    return e.matmul(bank[:, :],
                                    wcbb[:, slot * 2048 + ec * 128:
                                         slot * 2048 + (ec + 1) * 128],
                                    xnT[:, ec * S + sc * 512: ec * S + sc * 512 + 512],
                                    start=(ec == 0), stop=(ec == NEC - 1))
                v_mm = pe(f_mm, (sLD, v_w), w0ps_free[par], inc=(ec == NEC - 1))
            wcbb_free_b[slot] = ((sPE, sPE.n),)
            def f_gelu(e, bank=bank, j=j):
                e.activation(ar[:, HACT_O + j * 512:HACT_O + (j + 1) * 512],
                             bank[:, :], AF.Gelu, bias=b0S[:, j:j + 1])
            op("scalar", f_gelu, (sPE, v_mm), (sPE, hact_free_pe),
               (sST, y1sp_done) if sc == 0 and j == 0 else (sST, 0))
            w0ps_free[par] = (sAC, flush("scalar"))
        v_hact_ac = sAC.n

        for eoc in range(16):
            par = eoc % 2
            bank = ps[2 + par]
            for q in range(4):
                blk = 64 + eoc * 4 + q
                slot = (eoc * 4 + q) % 2
                v_w = dma("sync", wcbb[:, slot * 2048:(slot + 1) * 2048],
                          d_wbf[blk, :, :, :], *wcbb_free_b[slot],
                          (sST, WSP[blk]))
                for i in range(16):
                    ncb = q * 16 + i
                    def f_mm(e, bank=bank, slot=slot, i=i, ncb=ncb):
                        return e.matmul(bank[:, :],
                                        wcbb[:, slot * 2048 + i * 128:
                                             slot * 2048 + (i + 1) * 128],
                                        ar[:, HACT_O + ncb * 512:HACT_O + (ncb + 1) * 512],
                                        start=(ncb == 0), stop=(ncb == 63))
                    v_mm = pe(f_mm, (sLD, v_w), (sAC, v_hact_ac),
                              w1ps_free[par] if ncb == 0 else (sPE, 0),
                              inc=(i == 15 or ncb == 63))
                wcbb_free_b[slot] = ((sPE, sPE.n),)
            v_xt = dma("scalar", xtb[:, (eoc % 2) * 512:(eoc % 2) * 512 + 512],
                       d_xT[eoc * 128:(eoc + 1) * 128, sc * 512:sc * 512 + 512],
                       xtb_free[eoc % 2])
            y1o = Y1T_O + (eoc % 2) * 512
            v_y1 = dma("scalar", ar[:, y1o:y1o + 512],
                       d_y1sp[eoc * 128:(eoc + 1) * 128, sc * 512:sc * 512 + 512],
                       y1t_free[eoc % 2], (sPE, a_end_pe), (sST, y1sp_done))
            def f_f1(e, bank=bank, eoc=eoc, y1o=y1o):
                e.scalar_tensor_tensor(fst[:, 0:512], bank[:, :],
                                       b1S[:, eoc:eoc + 1], ar[:, y1o:y1o + 512],
                                       ALU.add, ALU.add)
            op("vector", f_f1, (sPE, v_mm), (sL2, v_y1))
            v_f1 = flush("vector")
            w1ps_free[par] = (sDV, v_f1)
            y1t_free[eoc % 2] = (sDV, v_f1)
            oslot = 512 + (eoc % 2) * 512
            def f_f2(e, eoc=eoc, oslot=oslot):
                e.tensor_tensor(fst[:, oslot:oslot + 512], fst[:, 0:512],
                                xtb[:, (eoc % 2) * 512:(eoc % 2) * 512 + 512], ALU.add)
            op("vector", f_f2, (sL2, v_xt), outst_free[eoc % 2])
            v_f2 = flush("vector")
            xtb_free[eoc % 2] = (sDV, v_f2)
            v_out = dma("gpsimd", d_out[eoc * 128:(eoc + 1) * 128, sc * 512:sc * 512 + 512],
                        fst[:, oslot:oslot + 512], (sDV, v_f2))
            outst_free[eoc % 2] = (sST, v_out)
        hact_free_pe = sPE.n

    def f_fin(e, v=sST.n):
        e.wait_ge(sST.h, v)
    prog["gpsimd"].append(((), f_fin))

    with cx:
        for s_ in sems:
            s_.h = cx.enter_context(nc.semaphore(s_.name))
        block = cx.enter_context(nc.Block())

        def run_stream(eng_name):
            def runner(e):
                hw = {}
                for waits, fn in prog[eng_name]:
                    for sem, val in waits:
                        if val > 0 and hw.get(sem.name, 0) < val:
                            e.wait_ge(sem.h, val)
                            hw[sem.name] = val
                    fn(e)
            return runner
        block.sync(run_stream("sync"))
        block.vector(run_stream("vector"))
        block.scalar(run_stream("scalar"))
        block.tensor(run_stream("tensor"))
        block.gpsimd(run_stream("gpsimd"))
    return nc


def _colblocks(WT, nb):
    K, N = WT.shape
    assert N == nb * 128
    return np.ascontiguousarray(WT.reshape(K // 128, 128, nb, 128).transpose(2, 1, 0, 3))


def _host_inputs(S, L0_attn_Wq, L0_attn_Wk, L0_attn_Wv, L0_attn_Wo,
                 L0_attn_q_norm_w, L0_attn_k_norm_w, L0_attn_Wrq, L0_attn_Wrk,
                 L0_attn_graph_scale, L0_ffn_W0, L0_ffn_bias0, L0_ffn_W1,
                 L0_ffn_bias1):
    f32 = np.float32
    base = {}
    base["wq"] = _colblocks(np.ascontiguousarray(L0_attn_Wq.T), 16)
    base["wk"] = _colblocks(np.ascontiguousarray(L0_attn_Wk.T), 4)
    base["wv"] = np.ascontiguousarray(L0_attn_Wv.T)
    base["wrq"] = _colblocks(np.ascontiguousarray(L0_attn_Wrq.T), 3)
    base["wrk"] = _colblocks(np.ascontiguousarray(L0_attn_Wrk.T), 3)
    base["wo"] = _colblocks(np.ascontiguousarray(L0_attn_Wo.T), 16)
    base["w0"] = _colblocks(np.ascontiguousarray(L0_ffn_W0.T), 64)
    base["w1"] = _colblocks(np.ascontiguousarray(L0_ffn_W1.T), 16)
    qw = np.asarray(L0_attn_q_norm_w, f32)
    kw = np.asarray(L0_attn_k_norm_w, f32)
    base["qnw"] = np.stack([qw, np.roll(qw, -64)], axis=1).astype(f32)
    base["knw"] = np.stack([kw, np.roll(kw, -64)], axis=1).astype(f32)
    expnt = np.arange(0, 128, 2, dtype=f32) / f32(128.0)
    invf = (f32(1.0) / np.power(f32(10000.0), expnt)).astype(f32)
    t = np.arange(S, dtype=f32)
    freqs = np.outer(t, invf).astype(f32)
    emb = np.concatenate([freqs, freqs], axis=1)
    base["costab"] = np.ascontiguousarray(np.cos(emb).T.astype(f32))
    base["sintab"] = np.ascontiguousarray(np.sin(emb).T.astype(f32))
    rotm = np.zeros((128, 128), np.float32)
    for d in range(64):
        rotm[d + 64, d] = -1.0
        rotm[d, d + 64] = 1.0
    base["rotm"] = rotm.astype(BFD)
    base["ident"] = np.eye(128, dtype=BFD)
    mp = np.arange(128)[:, None]
    mm_ = np.arange(896)[None, :]
    base["maskbuf"] = (mm_ >= mp + 384).astype(BFD)
    gs = np.asarray(L0_attn_graph_scale, f32)
    gs2 = np.zeros((128, 3), f32)
    for c in range(3):
        gs2[:, c] = gs[4 * c + np.arange(128) // 32]
    base["gs2"] = gs2
    base["b0r"] = np.ascontiguousarray(np.asarray(L0_ffn_bias0, f32).reshape(64, 128).T)
    base["b1r"] = np.ascontiguousarray(np.asarray(L0_ffn_bias1, f32).reshape(16, 128).T)
    cst = np.zeros((128, 10), f32)
    cst[:, 0] = -NEGC
    cst[:, 1] = 1e-5
    cst[:, 2] = 1e-6
    cst[:, 3] = 1.0 / 2048.0
    cst[:, 4] = 1.0 / 128.0
    cst[:, 5] = 0.5
    cst[:, 6] = 0.0
    cst[:, 7] = 1.0
    cst[:, 8] = -0.5
    cst[:, 9] = -1.0
    base["cst"] = cst
    base["onesc"] = np.ones((128, 1), BFD)
    base["onesr"] = np.ones((1, 128), BFD)
    base["onesm"] = np.ones((128, 128), BFD)
    return base


def kernel(L0_attn_x, L0_attn_Wq, L0_attn_Wk, L0_attn_Wv, L0_attn_Wo,
           L0_attn_q_norm_w, L0_attn_k_norm_w, L0_attn_Wrq, L0_attn_Wrk,
           L0_attn_graph_scale, L0_ffn_W0, L0_ffn_bias0, L0_ffn_W1, L0_ffn_bias1):
    S = L0_attn_x.shape[1]
    base = _host_inputs(S, L0_attn_Wq, L0_attn_Wk, L0_attn_Wv, L0_attn_Wo,
                        L0_attn_q_norm_w, L0_attn_k_norm_w, L0_attn_Wrq,
                        L0_attn_Wrk, L0_attn_graph_scale, L0_ffn_W0,
                        L0_ffn_bias0, L0_ffn_W1, L0_ffn_bias1)
    nc = build(S)
    B = L0_attn_x.shape[0]
    in_maps = []
    for c in range(B):
        m = dict(base)
        xc = np.ascontiguousarray(np.asarray(L0_attn_x[c], np.float32))
        m["x"] = xc
        m["xT"] = np.ascontiguousarray(xc.T)
        in_maps.append(m)
    res = run_bass_kernel_spmd(nc, in_maps, core_ids=list(range(B)))
    out = np.stack([np.ascontiguousarray(r["outT"].T) for r in res.results], axis=0)
    return out
